# version 04 (final)
# baseline (speedup 1.0000x reference)
import numpy as np
import ml_dtypes
import concourse.bass as bass
import concourse.mybir as mybir
from concourse.bass_utils import run_bass_kernel_spmd

F32 = mybir.dt.float32
BF16 = mybir.dt.bfloat16
AF = mybir.ActivationFunctionType
ALU = mybir.AluOpType
AXX = mybir.AxisListType.X

NCORES = 8
D = 1024
FF = 4096
KC = 8
FT = 32
TG = 256
NST = 16
NE = 8
YLAG = 3
HP = 5
EPS = 1e-5
DBG2 = False


def _sbmap():
    m = {}
    off = [16384]
    def add(name, nbytes, dt):
        m[name] = (off[0], nbytes, dt)
        off[0] += nbytes
    add("w1sb", 65536, BF16)
    add("w2sb", 65536, BF16)
    add("usb", 32768, BF16)
    add("xsb", 16384, F32)
    add("hbf", 5120, BF16)
    add("tmp", 2048, F32)
    add("mbcs", 2048, F32)
    add("mfm", 4096, BF16)
    add("sq16", 4096, BF16)
    add("bcsb", 2048, F32)
    add("rows", 4096, F32)
    add("rb", 1024, BF16)
    add("lgsb", 1024, F32)
    add("sel", 2048, BF16)
    add("onesc", 32, BF16)
    add("onesr", 256, BF16)
    add("identf", 512, F32)
    add("diffc", 512, F32)
    add("indc", 512, F32)
    add("masksb", 512, F32)
    add("ld0sb", 32, F32)
    add("rsm", 512, F32)
    add("b1sb", 256, F32)
    add("b2sb", 64, F32)
    add("swlsb", 128, BF16)
    add("swbsb", 32, F32)
    assert off[0] <= 228500, off[0]
    m["xb16b"] = (m["tmp"][0], 4096, BF16)
    m["gahb"] = (m["hbf"][0], 4096, F32)
    m["q_sb"] = (m["tmp"][0], 4096, BF16)
    m["k_sb"] = (m["mfm"][0], 4096, BF16)
    m["g1h"] = (m["sq16"][0], 4096, F32)
    m["z_sb"] = (m["sq16"][0], 4096, BF16)
    m["g_sb"] = (m["bcsb"][0], 4096, BF16)
    m["v_sb"] = (m["sel"][0], 2048, BF16)
    m["s_sb"] = (m["lgsb"][0] + 512, 512, BF16)
    return m


def sl3(ap2d, c, lo, hi):
    return ap2d.rearrange("p (c w) -> p c w", c=c)[:, :, lo:hi]


def build_program():
    nc = bass.Bass("TRN2", target_bir_lowering=False)

    xin = nc.dram_tensor("xin", [16, 128, KC * TG], F32, kind="ExternalInput").ap()
    w1 = nc.dram_tensor("w1", [NE, D, FF], F32, kind="ExternalInput").ap()
    w2 = nc.dram_tensor("w2", [NE, FF, D], F32, kind="ExternalInput").ap()
    b1l = nc.dram_tensor("b1l", [NE, 128, FT], F32, kind="ExternalInput").ap()
    b2l = nc.dram_tensor("b2l", [NE, 128, KC], F32, kind="ExternalInput").ap()
    swl_in = nc.dram_tensor("swli", [128, 64], F32, kind="ExternalInput").ap()
    swb_in = nc.dram_tensor("swbi", [8, 1], F32, kind="ExternalInput").ap()
    awT = {n: nc.dram_tensor(n, [D, D], F32, kind="ExternalInput").ap()
           for n in ("wqT", "wkT", "wvT", "wgT", "woT")}
    ld0_in = nc.dram_tensor("ld0i", [128, 1], F32, kind="ExternalInput").ap()
    onesc_in = nc.dram_tensor("onesci", [128, 1], BF16, kind="ExternalInput").ap()
    onesr_in = nc.dram_tensor("onesri", [128, 128], BF16, kind="ExternalInput").ap()
    sel_in = nc.dram_tensor("seli", [128, 1024], BF16, kind="ExternalInput").ap()
    identf_in = nc.dram_tensor("identfi", [128, 128], F32, kind="ExternalInput").ap()
    diff_in = nc.dram_tensor("diffi", [128, 128], F32, kind="ExternalInput").ap()
    ind_in = nc.dram_tensor("indi", [128, 128], F32, kind="ExternalInput").ap()

    x1s = nc.dram_tensor("x1s", [16, 128, KC * TG], F32, kind="Internal").ap()
    x2s = nc.dram_tensor("x2s", [16, 128, KC * TG], F32, kind="Internal").ap()
    x3s = nc.dram_tensor("x3s", [16, 128, KC * TG], F32, kind="Internal").ap()
    out = nc.dram_tensor("out", [16, 128, KC * TG], F32, kind="ExternalOutput").ap()
    dbg2 = (nc.dram_tensor("dbg2", [10, 128, 2048], BF16,
                           kind="ExternalOutput").ap() if DBG2 else None)
    dbgu = (nc.dram_tensor("dbgu", [128, 16384], BF16,
                           kind="ExternalOutput").ap() if DBG2 else None)
    dbgr = (nc.dram_tensor("dbgr", [4, 128, 1024], F32,
                           kind="ExternalOutput").ap() if DBG2 else None)

    SBM = _sbmap()
    from contextlib import ExitStack
    with ExitStack() as stk:
        E = stk.enter_context
        T = {}
        for name, (off, nbytes, dt) in SBM.items():
            esz = 2 if dt == BF16 else 4
            T[name] = nc.alloc_sbuf_tensor_at(name, [128, nbytes // esz], dt,
                                              offset=off)
        hps = E(nc.psum_tensor("hps", [128, 1024], F32))
        yps = E(nc.psum_tensor("yps", [128, 2048], F32))
        mbcps = E(nc.psum_tensor("mbcps", [128, 512], F32))
        rops = E(nc.psum_tensor("rops", [128, 512], F32))

        sem_names = ["wld", "cld", "bld", "xld", "xst", "hmm", "s0", "ymm",
                     "stt", "pad", "mbcmm", "mbce", "gt", "gs", "gv", "gp"]
        SEM = {n: E(nc.semaphore(n)) for n in sem_names}
        block = E(nc.Block())

        def walk(EN, e):
            cnt = {n: 0 for n in sem_names}
            mark = {}
            xrel = {}
            S8 = {"xli": 0, "ei": 0, "ust": 0, "yinfo": {}, "lastustt": 0,
                  "lastpad": 0, "prevunit": None, "uq": 0}

            def I(engine, f, sem=None, key=None, n=1, waits=()):
                if EN == engine:
                    for s, v in waits:
                        if v:
                            e.wait_ge(SEM[s], v)
                    ins = f()
                    if sem is not None and ins is not None:
                        ins.then_inc(SEM[sem], n)
                if sem is not None:
                    cnt[sem] += n
                    if key is not None:
                        mark[key] = cnt[sem]

            def M(key):
                return mark.get(key, 0)

            def const_loads():
                pairs = [(T["swbsb"][0:8, 0:1], swb_in[:, :]),
                         (T["onesc"][:, 0:1], onesc_in[:, :]),
                         (T["onesr"][:, :], onesr_in[:, :]),
                         (T["sel"][:, :], sel_in[:, :]),
                         (T["identf"][:, :], identf_in[:, :]),
                         (T["diffc"][:, :], diff_in[:, :]),
                         (T["indc"][:, :], ind_in[:, :]),
                         (T["ld0sb"][:, 0:1], ld0_in[:, :])]
                for dst, src in pairs:
                    I("q", lambda dst=dst, src=src: e.dma_start(out=dst, in_=src),
                      sem="cld", n=16)
                mark[("cld0",)] = cnt["cld"]
                I("p", lambda: e.dma_start(out=T["swlsb"][:, :], in_=swl_in[:, :]),
                  sem="wld", key=("swl",), n=16)
                I("p", lambda: e.memset(T["ld0sb"][:, 5:6], EPS),
                  sem="gp", key=("eps",))
                I("p", lambda: e.memset(T["ld0sb"][:, 6:7], 0.0),
                  sem="gp", key=("zb0",))
                I("p", lambda: e.memset(T["rb"][:, :], 0.0),
                  sem="gp", key=("rb0",))
                I("p", lambda: e.memset(T["mfm"][:, :], 0.0),
                  sem="gp", key=("mfm0",))

            def ln_block(pid, half, g, xs, xb, key, uslot, extraT=(), extraV=(),
                         extraS=(), extraP=()):
                I("v", lambda: e.tensor_tensor(T["sq16"][:, :], xs, xs, ALU.mult),
                  sem="gv", key=("Asq",) + key,
                  waits=[("xld", M(("xl",) + key)),
                         ("gt", M(("Ast",) + key[:-1] + (key[-1] - 1,)))]
                  + list(extraS))

                def stats():
                    for k in range(KC):
                        e.matmul(rops[0:1, 0:256], T["onesc"][:, 0:1],
                                 xb[:, k * 256:(k + 1) * 256],
                                 start=(k == 0), stop=(k == KC - 1))
                    ins = None
                    for k in range(KC):
                        ins = e.matmul(rops[0:1, 256:512], T["onesc"][:, 0:1],
                                       T["sq16"][:, k * 256:(k + 1) * 256],
                                       start=(k == 0), stop=(k == KC - 1))
                    return ins
                I("t", stats, sem="gt", key=("Ast",) + key,
                  waits=[("gv", M(("Axb",) + key)), ("gv", M(("Asq",) + key)),
                         ("gs", M(("Abce",) + key[:-1] + (key[-1] - 1,))),
                         ("cld", M(("cld0",)))] + list(extraT))

                rA = T["rows"][0:1, 0:256]
                rB = T["rows"][0:1, 256:512]
                rC = T["rows"][0:1, 512:768]
                rD = T["rows"][0:1, 768:1024]

                def rowse():
                    e.activation(rA, rops[0:1, 0:256], AF.Identity,
                                 bias=T["ld0sb"][0:1, 6:7])
                    return e.activation(rB, rops[0:1, 256:512], AF.Identity,
                                        bias=T["ld0sb"][0:1, 6:7])
                I("s", rowse, sem="gs", key=("Arw",) + key,
                  waits=[("gt", M(("Ast",) + key)),
                         ("gt", M(("Abc",) + key[:-1] + (key[-1] - 1,)))])
                I("v", lambda: e.scalar_tensor_tensor(
                    rC, rA, 1.0 / 1024.0, rA, ALU.mult, ALU.mult),
                  sem="gv", key=("Am2",) + key,
                  waits=[("gs", M(("Arw",) + key))])
                I("p", lambda: e.tensor_tensor(rD, rB, rC, ALU.subtract),
                  sem="gp", key=("Avr",) + key,
                  waits=[("gv", M(("Am2",) + key))])
                I("s", lambda: e.activation(rC, rD, AF.Ln,
                                            bias=T["ld0sb"][0:1, 5:6],
                                            scale=1.0 / 1024.0),
                  sem="gs", key=("Aln",) + key,
                  waits=[("gp", M(("Avr",) + key)), ("gp", M(("eps",)))])
                I("s", lambda: e.activation(rB, rC, AF.Exp, scale=-0.5,
                                            bias=T["ld0sb"][0:1, 6:7]),
                  sem="gs", key=("Ars",) + key,
                  waits=[("gs", M(("Aln",) + key))])
                I("v", lambda: e.scalar_tensor_tensor(
                    rC, rA, 1.0 / 1024.0, rB, ALU.mult, ALU.mult),
                  sem="gv", key=("Amr",) + key,
                  waits=[("gs", M(("Ars",) + key))])

                def rbc():
                    e.tensor_copy(T["rb"][0:1, 0:256], rB)
                    return e.tensor_copy(T["rb"][0:1, 256:512], rC)
                I("p", rbc, sem="gp", key=("Arb",) + key,
                  waits=[("gv", M(("Amr",) + key)), ("gp", M(("rb0",)))])

                def bc():
                    e.matmul(hps[:, 0:256], T["onesr"][:, :],
                             T["rb"][:, 0:256], start=True, stop=True)
                    return e.matmul(hps[:, 256:512], T["onesr"][:, :],
                                    T["rb"][:, 256:512], start=True, stop=True)
                I("t", bc, sem="gt", key=("Abc",) + key,
                  waits=[("gp", M(("Arb",) + key)), ("gs", M(("Arw",) + key))])
                I("s", lambda: e.activation(T["bcsb"][:, :], hps[:, 0:512],
                                            AF.Identity,
                                            bias=T["ld0sb"][:, 6:7]),
                  sem="gs", key=("Abce",) + key,
                  waits=[("gt", M(("Abc",) + key)),
                         ("gp", M(("Au",) + key[:-1] + (key[-1] - 1, KC - 1))),
                         ("gp", M(("zb0",)))])
                for k in range(KC):
                    wv = [("gs", M(("Abce",) + key)),
                          ("gp", M(("Au",) + key + (k - 1,)) if k else
                           M(("Au",) + key[:-1] + (key[-1] - 1, KC - 1)))]
                    if k == 0:
                        wv += list(extraV)
                    I("v", lambda k=k: e.tensor_tensor(
                        T["tmp"][:, 0:256], xs[:, k * 256:(k + 1) * 256],
                        T["bcsb"][:, 0:256], ALU.mult),
                      sem="gv", key=("At",) + key + (k,), waits=wv)
                    wp = [("gv", M(("At",) + key + (k,)))]
                    if k == 0:
                        wp += list(extraP)
                    I("p", lambda k=k: e.tensor_tensor(
                        T["usb"][:, k * 2048 + uslot * 256:
                                 k * 2048 + uslot * 256 + 256],
                        T["tmp"][:, 0:256], T["bcsb"][:, 256:512], ALU.subtract),
                      sem="gp", key=("Au",) + key + (k,), waits=wp)

            def phaseA(pid, half):
                xsrc = xin if pid == 1 else x2s
                lasth = M(("lasth",)); lasty = M(("lasty",))
                lasts0 = M(("lasts0",))
                att = M(("attend",)); attv = M(("attendv",))
                p3b = (pid == 3 and half == 0)
                for g in range(8):
                    key = (pid, half, g)
                    gidx = half * 8 + g
                    xli = S8["xli"]; S8["xli"] += 1
                    rel = xrel.get(xli - 2)
                    xs = T["xsb"][:, (xli % 2) * 2048:(xli % 2) * 2048 + 2048]
                    I("q", lambda xs=xs, gidx=gidx, xsrc=xsrc:
                        e.dma_start(out=xs, in_=xsrc[gidx, :, :]),
                      sem="xld", key=("xl",) + key, n=16,
                      waits=([rel] if rel else []))
                    xb = (T["hbf"][:, 0:2048] if xli % 2 == 0 else T["xb16b"][:, :])
                    wx = [("xld", M(("xl",) + key)),
                          ("gt", M(("Ast", pid, half, g - 2)))]
                    if g < 2:
                        wx += [("ymm", lasty), ("s0", lasts0),
                               ("pad", S8["lastpad"]), ("stt", S8["lastustt"])]
                        if p3b:
                            wx += [("gt", att), ("gv", attv)]
                    I("v", lambda xb=xb, xs=xs: e.tensor_copy(xb, xs),
                      sem="gv", key=("Axb",) + key, waits=wx)
                    eT = [("gt", att)] if (p3b and g == 0) else []
                    if DBG2 and pid == 1 and half == 1 and g == 1:
                        eT = eT + [("xst", M(("dbgr3",)))]
                    if DBG2 and pid == 1 and half == 0 and g == 1:
                        eT = eT + [("xst", M(("dbgh0",)))]
                    eS = [("gt", att)] if (p3b and g == 0) else []
                    eV = ([("pad", S8["lastpad"]), ("gt", att)]
                          if g == 0 else [])
                    eP = ([("hmm", lasth), ("gt", att)] if g == 0 else [])
                    ln_block(pid, half, g, xs, xb, key, g, extraT=eT, extraV=eV,
                             extraS=eS, extraP=eP)
                    if DBG2 and pid == 1 and half == 0 and g == 0:
                        I("q", lambda: e.dma_start(out=dbgr[0, :, 0:512],
                                                   in_=T["bcsb"][:, :]),
                          sem="xst", key=("dbgh0",), n=16,
                          waits=[("gp", M(("Au",) + key + (KC - 1,)))])
                    if DBG2 and pid == 1 and half == 1 and g == 0:
                        I("q", lambda: e.dma_start(out=dbgr[2, :, 0:512],
                                                   in_=T["bcsb"][:, :]),
                          sem="xst", key=("dbgr2",), n=16,
                          waits=[("gp", M(("Au",) + key + (KC - 1,)))])
                        I("q", lambda: e.dma_start(out=dbgr[3, :, :],
                                                   in_=T["rows"][:, :]),
                          sem="xst", key=("dbgr3",), n=16)
                    xrel[xli] = ("gv", M(("At",) + key + (KC - 1,)))

                    def lgmm(g=g):
                        ins = None
                        for k in range(KC):
                            ins = e.matmul(hps[0:8, 0:256],
                                           T["swlsb"][:, k * 8:(k + 1) * 8],
                                           T["usb"][:, k * 2048 + g * 256:
                                                    k * 2048 + g * 256 + 256],
                                           start=(k == 0), stop=(k == KC - 1))
                        return ins
                    wl = [("gp", M(("Au",) + key + (KC - 1,))),
                          ("wld", M(("swl",)))]
                    if g == 0:
                        wl.append(("s0", lasts0))
                    I("t", lgmm, sem="gt", key=("Alg",) + key, waits=wl)
                    I("s", lambda: e.activation(T["lgsb"][0:8, 0:128],
                                                hps[0:8, 0:128], AF.Identity,
                                                bias=T["swbsb"][0:8, 0:1]),
                      sem="gs", key=("Alg1",) + key,
                      waits=[("gt", M(("Alg",) + key)),
                             ("gt", M(("Amf", pid, half, g - 1)))])
                    I("s", lambda: e.activation(T["lgsb"][0:8, 128:256],
                                                hps[0:8, 128:256], AF.Identity,
                                                bias=T["swbsb"][0:8, 0:1]),
                      sem="gs", key=("Alge",) + key, waits=[])

                    def ltt():
                        e.transpose(hps[:, 512:520], T["lgsb"][0:8, 0:128],
                                    T["identf"][0:8, 0:8])
                        return e.transpose(hps[:, 520:528], T["lgsb"][0:8, 128:256],
                                           T["identf"][0:8, 0:8])
                    I("t", ltt, sem="gt", key=("Alt",) + key,
                      waits=[("gs", M(("Alge",) + key))])

                    def nmx():
                        e.reduce_max(T["rsm"][:, 0:1], hps[:, 512:520], axis=AXX,
                                     negate=True)
                        return e.reduce_max(T["rsm"][:, 1:2], hps[:, 520:528],
                                            axis=AXX, negate=True)
                    I("v", nmx, sem="gv", key=("Anx",) + key,
                      waits=[("gt", M(("Alt",) + key)),
                             ("gv", M(("Aid", pid, half, g - 1)))])

                    def exs():
                        e.activation(T["rsm"][:, 8:16], hps[:, 512:520], AF.Exp,
                                     bias=T["rsm"][:, 0:1])
                        e.activation(T["rsm"][:, 16:24], hps[:, 520:528], AF.Exp,
                                     bias=T["rsm"][:, 1:2])
                        e.activation(T["rsm"][:, 24:32], hps[:, 512:520],
                                     AF.Identity, bias=T["ld0sb"][:, 6:7])
                        return e.activation(T["rsm"][:, 32:40], hps[:, 520:528],
                                            AF.Identity,
                                            bias=T["ld0sb"][:, 6:7])
                    I("s", exs, sem="gs", key=("Aex",) + key,
                      waits=[("gv", M(("Anx",) + key)),
                             ("gv", M(("Ase", pid, half, g - 1))),
                             ("gv", M(("Aid", pid, half, g - 1)))])

                    def ses():
                        e.reduce_sum(T["rsm"][:, 2:3], T["rsm"][:, 8:16], axis=AXX)
                        return e.reduce_sum(T["rsm"][:, 3:4], T["rsm"][:, 16:24],
                                            axis=AXX)
                    I("v", ses, sem="gv", key=("Ase",) + key,
                      waits=[("gs", M(("Aex",) + key))])

                    def rcs():
                        e.reciprocal(T["rsm"][:, 4:5], T["rsm"][:, 2:3])
                        return e.reciprocal(T["rsm"][:, 5:6], T["rsm"][:, 3:4])
                    I("v", rcs, sem="gv", key=("Arc",) + key,
                      waits=[("gv", M(("Ase",) + key))])

                    def inds():
                        e.tensor_scalar(T["rsm"][:, 40:48], T["rsm"][:, 24:32],
                                        T["rsm"][:, 0:1], 0.0, ALU.add,
                                        ALU.is_equal)
                        return e.tensor_scalar(T["rsm"][:, 48:56],
                                               T["rsm"][:, 32:40],
                                               T["rsm"][:, 1:2], 0.0, ALU.add,
                                               ALU.is_equal)
                    I("v", inds, sem="gv", key=("Aid",) + key,
                      waits=[("gs", M(("Aex",) + key))])

                    def mtoks():
                        e.tensor_scalar(T["rsm"][:, 56:64], T["rsm"][:, 40:48],
                                        T["rsm"][:, 4:5], 0.5, ALU.mult, ALU.mult)
                        return e.tensor_scalar(T["rsm"][:, 64:72],
                                               T["rsm"][:, 48:56],
                                               T["rsm"][:, 5:6], 0.5, ALU.mult,
                                               ALU.mult)
                    I("v", mtoks, sem="gv", key=("Amt",) + key,
                      waits=[("gv", M(("Aid",) + key)),
                             ("gv", M(("Arc",) + key)),
                             ("gt", M(("Amf", pid, half, g - 1)))])

                    def mft():
                        e.transpose(yps[0:8, 0:128], T["rsm"][:, 56:64],
                                    T["identf"][:, :])
                        return e.transpose(yps[0:8, 512:640], T["rsm"][:, 64:72],
                                           T["identf"][:, :])
                    wm = [("gv", M(("Amt",) + key))]
                    if g == 0:
                        wm += [("ymm", lasty), ("stt", S8["lastustt"])]
                        if p3b:
                            wm.append(("gv", attv))
                    I("t", mft, sem="gt", key=("Amf",) + key, waits=wm)

                    def mfe(g=g):
                        e.activation(T["mfm"][0:8, g * 256:g * 256 + 128],
                                     yps[0:8, 0:128], AF.Identity,
                                     bias=T["ld0sb"][0:8, 6:7])
                        return e.activation(T["mfm"][0:8, g * 256 + 128:
                                                     g * 256 + 256],
                                            yps[0:8, 512:640], AF.Identity,
                                            bias=T["ld0sb"][0:8, 6:7])
                    we = [("gt", M(("Amf",) + key))]
                    if g == 0:
                        we.append(("mbcmm", cnt["mbcmm"]))
                        we.append(("gp", M(("mfmr", pid)) or M(("mfm0",))))
                    I("s", mfe, sem="gs", key=("Ame",) + key, waits=we)

            def phaseB(pid, half):
                eorder = (list(range(NE)) if half == 0
                          else list(range(NE - 1, -1, -1)))
                src_first = xin if pid == 1 else x2s
                scr = x1s if pid == 1 else x3s
                NLOC = NE * 8 * NST
                base_u = S8["ust"]
                pB_lastustt = S8["lastustt"]

                def expert_pre(en, ex):
                    ei = S8["ei"]
                    mark[("uend", ei)] = S8["ust"]
                    prev = S8["ust"]
                    for kb in range(KC):
                        wv = []
                        if kb == 0 and ei > 0:
                            wv.append(("hmm", prev))
                            if pid == 3 and en == 0 and half == 0:
                                wv.append(("gt", M(("attend",))))
                        I("p", lambda ex=ex, kb=kb: e.dma_start(
                            out=T["w1sb"][:, kb * FF:(kb + 1) * FF],
                            in_=w1[ex, kb * 128:(kb + 1) * 128, :]),
                          sem="wld", key=("w1", ei, kb), n=16, waits=wv)
                    mark[("w1f", ei)] = cnt["wld"]
                    for fb in range(4):
                        wv = []
                        if fb == 0 and ei > 0:
                            wv.append(("ymm", prev))
                            if pid == 3 and en == 0 and half == 0:
                                wv.append(("gt", M(("attend",))))
                        I("p", lambda ex=ex, fb=fb: e.dma_start(
                            out=sl3(T["w2sb"][:, fb * 8 * D:(fb + 1) * 8 * D], 8,
                                    0, D),
                            in_=w2[ex, fb * 1024:(fb + 1) * 1024, :].rearrange(
                                "(c p) d -> p c d", p=128)),
                          sem="wld", key=("w2", ei, fb), n=16, waits=wv)
                    mark[("w2f", ei)] = cnt["wld"]
                    bpar = ei % 2
                    I("q", lambda ex=ex, bpar=bpar: e.dma_start(
                        out=T["b1sb"][:, bpar * 32:(bpar + 1) * 32],
                        in_=b1l[ex, :, :]),
                      sem="bld", key=("b1", ei), n=16,
                      waits=[("s0", M(("uend", ei - 1)))])
                    I("q", lambda ex=ex, bpar=bpar: e.dma_start(
                        out=T["b2sb"][:, bpar * 8:(bpar + 1) * 8],
                        in_=b2l[ex, :, :]),
                      sem="bld", key=("b2", ei), n=16,
                      waits=[("stt", max(0, 64 * (ei - 1)))])

                def grp_pre(en, ex, g):
                    if g == 0:
                        S8["ei"] += 1
                    ei = S8["ei"] - 1
                    unit = (pid, half, en, g)
                    gidx = half * 8 + g
                    xli = S8["xli"]; S8["xli"] += 1
                    mark[("uxli", unit)] = xli
                    rel = xrel.get(xli - 2)
                    xs = T["xsb"][:, (xli % 2) * 2048:(xli % 2) * 2048 + 2048]
                    src = src_first if en == 0 else scr
                    I("q", lambda xs=xs, src=src, gidx=gidx:
                        e.dma_start(out=xs, in_=src[gidx, :, :]),
                      sem="xld", key=("xl",) + unit, n=16,
                      waits=([rel] if rel else []))
                    wmb = [("mbce", cnt["mbce"]),
                           ("gs", M(("Ame", pid, half, g)))]
                    if pid == 3 and en == 0 and g == 0 and half == 0:
                        wmb.append(("gv", M(("attendv",))))
                    if pid == 3:
                        wmb.append(("cld", M(("selr",))))
                    I("t", lambda ex=ex, g=g: e.matmul(
                        mbcps[:, 0:256], T["sel"][:, ex * 128:(ex + 1) * 128],
                        T["mfm"][:, g * 256:(g + 1) * 256], start=True,
                        stop=True),
                      sem="mbcmm", key=("mbcm",) + unit, waits=wmb)
                    uq = S8["uq"]; S8["uq"] += 1
                    wme = [("mbcmm", M(("mbcm",) + unit)),
                           ("stt", max(0, 8 * (uq - 1)))]
                    if en == 0 and g == 0:
                        wme.append(("gt", M(("Ast", pid, half, 7))))
                    I("s", lambda xli=xli: e.activation(
                        T["mbcs"][:, (xli % 2) * 256:(xli % 2) * 256 + 256],
                        mbcps[:, 0:256], AF.Identity,
                        bias=T["ld0sb"][:, 6:7]),
                      sem="mbce", key=("mbce",) + unit, waits=wme)
                    return unit, xli, xs, ei

                def emit_y(yu):
                    (en, ex, g, fg, ei, bpar, xli, unit) = S8["yinfo"][yu]

                    def ymm():
                        ins = None
                        for i in range(2):
                            ft = 2 * fg + i
                            for d in range(KC):
                                ins = e.matmul(
                                    yps[:, d * 256:(d + 1) * 256],
                                    T["w2sb"][:, ft * D + d * 128:
                                              ft * D + (d + 1) * 128],
                                    T["hbf"][:, (yu % HP) * 512 + i * 256:
                                             (yu % HP) * 512 + (i + 1) * 256],
                                    start=(ft == 0 and d % 2 == 0),
                                    stop=(ft == FT - 1 and d % 2 == 1))
                        return ins
                    yw = [("s0", M(("s0", yu)))]
                    if fg == 0:
                        yw.append(("stt", S8["lastustt"]))
                        if g == 0:
                            yw.append(("wld", M(("w2f", ei))))
                    I("t", ymm, sem="ymm", key=("y", yu), waits=yw)

                def emit_evac(yu_last):
                    (en, ex, g, fg, ei, bpar, xli, unit) = S8["yinfo"][yu_last]
                    gidx = half * 8 + g
                    xs = T["xsb"][:, (xli % 2) * 2048:(xli % 2) * 2048 + 2048]
                    pu = S8["prevunit"]
                    for d in range(KC):
                        ws = [("pad", M(("pad",) + unit + (d - 2,)))]
                        if d == 0:
                            ws += [("ymm", M(("y", yu_last))),
                                   ("mbce", M(("mbce",) + unit)),
                                   ("xld", M(("xl",) + unit)),
                                   ("pad", M(("pad",) + (pu or ("-",)) + (6,)))]
                            if en == 0 and g == 0:
                                ws.append(("gp", M(("Au", pid, half, 7, KC - 1))))
                                if pid == 3 and half == 0:
                                    ws.append(("gt", M(("attend",))))
                        if d == 1:
                            ws.append(("pad", M(("pad",) + (pu or ("-",)) + (7,))))
                        I("v", lambda d=d, bpar=bpar, xli=xli, xs=xs:
                            e.scalar_tensor_tensor(
                                T["tmp"][:, (d % 2) * 256:(d % 2) * 256 + 256],
                                yps[:, d * 256:(d + 1) * 256],
                                T["b2sb"][:, bpar * 8 + d: bpar * 8 + d + 1],
                                T["mbcs"][:, (xli % 2) * 256:
                                          (xli % 2) * 256 + 256],
                                ALU.add, ALU.mult),
                          sem="stt", key=("stt",) + unit + (d,), waits=ws)
                        I("p", lambda d=d, xs=xs: e.tensor_tensor(
                            xs[:, d * 256:(d + 1) * 256],
                            xs[:, d * 256:(d + 1) * 256],
                            T["tmp"][:, (d % 2) * 256:(d % 2) * 256 + 256],
                            ALU.add),
                          sem="pad", key=("pad",) + unit + (d,),
                          waits=[("stt", M(("stt",) + unit + (d,)))])
                    mark[("ustt", xli)] = M(("stt",) + unit + (KC - 1,))
                    S8["lastustt"] = M(("stt",) + unit + (KC - 1,))
                    S8["lastpad"] = M(("pad",) + unit + (KC - 1,))
                    S8["prevunit"] = unit
                    dst = out if (pid == 3 and en == NE - 1) else scr
                    I("q", lambda dst=dst, gidx=gidx, xs=xs:
                        e.dma_start(out=dst[gidx, :, :], in_=xs),
                      sem="xst", key=("xs",) + unit, n=16,
                      waits=[("pad", M(("pad",) + unit + (KC - 1,)))])
                    xrel[xli] = ("xst", M(("xs",) + unit))

                cur = {}
                for ls in range(NLOC + YLAG + 1):
                    if ls < NLOC:
                        en = ls // 128
                        g = (ls // NST) % 8
                        fg = ls % NST
                        ex = eorder[en]
                        if fg == 0 and g == 0:
                            expert_pre(en, ex)
                        if fg == 0:
                            cur["u"] = grp_pre(en, ex, g)
                        unit, xli, xs, ei = cur["u"]
                        bpar = ei % 2
                        u = S8["ust"]

                        def hmm(g=g, fg=fg, u=u):
                            ins = None
                            for i in range(2):
                                ft = 2 * fg + i
                                for k in range(KC):
                                    ins = e.matmul(
                                        hps[:, (u % 2) * 512 + i * 256:
                                            (u % 2) * 512 + (i + 1) * 256],
                                        T["w1sb"][:, k * FF + ft * 128:
                                                  k * FF + (ft + 1) * 128],
                                        T["usb"][:, k * 2048 + g * 256:
                                                 k * 2048 + g * 256 + 256],
                                        start=(k == 0), stop=(k == KC - 1))
                            return ins
                        hw = [("s0", M(("s0", u - 2)))]
                        if fg == 0 and g == 0:
                            hw.append(("wld", M(("w1f", ei))))
                            hw.append(("gp", M(("Au", pid, half, 7, KC - 1))))
                        I("t", hmm, sem="hmm", key=("h", u), waits=hw)

                        def gel(fg=fg, bpar=bpar, u=u):
                            ins = None
                            for i in range(2):
                                ft = 2 * fg + i
                                ins = e.activation(
                                    T["hbf"][:, (u % HP) * 512 + i * 256:
                                             (u % HP) * 512 + (i + 1) * 256],
                                    hps[:, (u % 2) * 512 + i * 256:
                                        (u % 2) * 512 + (i + 1) * 256],
                                    AF.Gelu_apprx_tanh,
                                    bias=T["b1sb"][:, bpar * 32 + ft:
                                                   bpar * 32 + ft + 1])
                            return ins
                        gw = [("hmm", M(("h", u))), ("ymm", M(("y", u - HP + 1)))]
                        if fg == 0 and g == 0:
                            gw.append(("bld", M(("b1", ei))))
                            if en == 0:
                                gw.append(("gt", M(("Ast", pid, half, 7))))
                        I("s", gel, sem="s0", key=("s0", u), waits=gw)
                        S8["yinfo"][u] = (en, ex, g, fg, ei, bpar, xli, unit)
                        S8["ust"] += 1
                    yls = ls - YLAG
                    if 0 <= yls < NLOC:
                        yu = base_u + yls
                        emit_y(yu)
                        if yls % NST == NST - 1:
                            emit_evac(yu)
                mark[("lasth",)] = cnt["hmm"]
                mark[("lasty",)] = cnt["ymm"]
                mark[("lasts0",)] = cnt["s0"]

            def pass2():
                lh = M(("lasth",)); ly = M(("lasty",)); ls0 = M(("lasts0",))
                lstt = S8["lastustt"]; lpad = S8["lastpad"]
                p2mb = cnt["mbcmm"]
                if DBG2:
                    I("q", lambda: e.dma_start(out=dbgu[:, :], in_=T["usb"][:, :]),
                      sem="xst", key=("dbgu",), n=16,
                      waits=[("ymm", ly)])
                aws = [("wqT", "w1sb", 0), ("wkT", "w1sb", 8192),
                       ("wvT", "w1sb", 16384), ("wgT", "w1sb", 24576),
                       ("woT", "w2sb", 0)]
                for i, (nm, dstb, woff) in enumerate(aws):
                    wv = []
                    if i == 0:
                        wv = [("hmm", lh)]
                    if nm == "woT":
                        wv = [("ymm", ly)]
                    I("p", lambda nm=nm, dstb=dstb, woff=woff: e.dma_start(
                        out=sl3(T[dstb][:, woff:woff + 8192], 8, 0, D),
                        in_=awT[nm].rearrange("(c p) d -> p c d", p=128)),
                      sem="wld", key=("aw", nm), n=16, waits=wv)
                I("s", lambda: e.activation(T["ld0sb"][:, 1:2], T["ld0sb"][:, 0:1],
                                            AF.Exp, scale=-1.0,
                                            bias=T["ld0sb"][:, 6:7]),
                  sem="gs", key=("mk1",),
                  waits=[("cld", M(("cld0",))), ("s0", ls0),
                         ("gp", M(("zb0",)))])
                I("v", lambda: e.tensor_scalar(T["ld0sb"][:, 2:3],
                                               T["ld0sb"][:, 1:2], 1.0, None,
                                               ALU.add),
                  sem="gv", key=("mk2",), waits=[("gs", M(("mk1",)))])
                I("v", lambda: e.reciprocal(T["ld0sb"][:, 3:4], T["ld0sb"][:, 2:3]),
                  sem="gv", key=("mk3",), waits=[("gv", M(("mk2",)))])
                I("s", lambda: e.activation(T["ld0sb"][:, 4:5], T["ld0sb"][:, 3:4],
                                            AF.Ln, bias=T["ld0sb"][:, 6:7]),
                  sem="gs", key=("mk4",), waits=[("gv", M(("mk3",)))])
                I("s", lambda: e.activation(T["masksb"][:, :], T["diffc"][:, :],
                                            AF.Exp, scale=T["ld0sb"][:, 4:5],
                                            bias=T["ld0sb"][:, 6:7]),
                  sem="gs", key=("mk5",), waits=[("gs", M(("mk4",)))])
                I("v", lambda: e.tensor_tensor(T["masksb"][:, :], T["masksb"][:, :],
                                               T["indc"][:, :], ALU.mult),
                  sem="gv", key=("mk6",), waits=[("gs", M(("mk5",)))])

                for gidx in range(16):
                    half = gidx // 8
                    key = (2, 0, gidx)
                    uslot = gidx % 2
                    xli = S8["xli"]; S8["xli"] += 1
                    rel = xrel.get(xli - 2)
                    xs = T["xsb"][:, (xli % 2) * 2048:(xli % 2) * 2048 + 2048]
                    I("q", lambda xs=xs, gidx=gidx:
                        e.dma_start(out=xs, in_=x1s[gidx, :, :]),
                      sem="xld", key=("xl",) + key, n=16,
                      waits=([rel] if rel else []))
                    xb = (T["hbf"][:, 0:2048] if xli % 2 == 0 else T["xb16b"][:, :])
                    wx = [("xld", M(("xl",) + key)),
                          ("gv", M(("b2e", gidx - 1))),
                          ("gt", M(("aomm", gidx - 1, 1)))]
                    if gidx < 2:
                        wx += [("ymm", ly), ("s0", ls0), ("pad", lpad),
                               ("stt", lstt)]
                    I("v", lambda xb=xb, xs=xs: e.tensor_copy(xb, xs),
                      sem="gv", key=("Axb",) + key, waits=wx)
                    eT = [("stt", lstt)] if gidx == 0 else []
                    eV = ([("pad", lpad)] if gidx == 0 else []) + \
                        [("gt", M(("aomm", gidx - 1, 1)))]
                    eP = ([("hmm", lh)] if gidx == 0 else []) + \
                        [("gt", M(("vmm", gidx - 2, 1)))]
                    if DBG2 and gidx < 2:
                        eP = eP + [("xst", M(("dbgu",)))]
                    ln_block(2, half, gidx, xs, xb, key, uslot, extraT=eT,
                             extraV=eV, extraP=eP)
                    if DBG2 and gidx == 0:
                        I("q", lambda: e.dma_start(out=dbgr[0, :, 0:512],
                                                   in_=T["bcsb"][:, :]),
                          sem="xst", key=("dbgr0",), n=16,
                          waits=[("gs", M(("Abce",) + key))])
                        I("q", lambda: e.dma_start(out=dbgr[1, :, :],
                                                   in_=T["rows"][:, :]),
                          sem="xst", key=("dbgr1",), n=16)
                    mark[("uxli2", gidx)] = xli
                    b2_grp(gidx, half, key, uslot, xli, xs, p2mb, ls0, lstt, lpad)
                mark[("attend",)] = cnt["gt"]
                mark[("attendv",)] = cnt["gv"]
                I("p", lambda: e.memset(T["rb"][:, :], 0.0),
                  sem="gp", key=("rb1",), waits=[("gv", cnt["gv"])])
                I("p", lambda: e.memset(T["mfm"][:, :], 0.0),
                  sem="gp", key=("mfmr", 3), waits=[("gt", cnt["gt"])])
                I("q", lambda: e.dma_start(out=T["sel"][:, :], in_=sel_in[:, :]),
                  sem="cld", key=("selr",), n=16, waits=[("gt", cnt["gt"])])

            def b2_grp(gidx, half, key, uslot, xli, xs, p2mb, ls0, lstt, lpad):
                def u_ap(k):
                    return T["usb"][:, k * 2048 + uslot * 256:
                                    k * 2048 + uslot * 256 + 256]
                def u_tile(k, t):
                    base = k * 2048 + uslot * 256 + t * 128
                    return T["usb"][:, base:base + 128]
                for nm, woff, dstb, ps in (("q", 0, "q_sb", hps),
                                           ("k", 8192, "k_sb", yps)):
                    for r in range(2):
                        def pmm(nm=nm, woff=woff, r=r, ps=ps):
                            ins = None
                            for dj in range(4):
                                dd = r * 4 + dj
                                for k in range(KC):
                                    ins = e.matmul(
                                        ps[:, dj * 256:(dj + 1) * 256],
                                        T["w1sb"][:, woff + k * 1024 + dd * 128:
                                                  woff + k * 1024 +
                                                  (dd + 1) * 128],
                                        u_ap(k), start=(k == 0),
                                        stop=(k == KC - 1))
                            return ins
                        ws = [("gp", M(("Au",) + key + (KC - 1,)))]
                        if r == 1:
                            ws.append(("gs", M((nm + "ev", gidx, 0))))
                        if nm == "q":
                            ws.append(("gv", M(("aoadd", gidx - 1, 0))))
                            if gidx == 0:
                                ws.append(("wld", M(("aw", "wgT"))))
                                ws.append(("s0", ls0))
                        else:
                            ws.append(("gv", M(("zz", gidx - 1, 1))))
                            ws.append(("gv", M(("aoadd", gidx - 1, 1))))
                        I("t", pmm, sem="gt", key=(nm + "mm", gidx, r), waits=ws)
                        wse = [("gt", M((nm + "mm", gidx, r)))]
                        if gidx == 0 and r == 0:
                            wse += [("stt", lstt), ("mbcmm", p2mb),
                                    ("pad", lpad)]
                        I("s", lambda nm=nm, dstb=dstb, r=r, ps=ps: e.activation(
                            T[dstb][:, r * 1024:(r + 1) * 1024], ps[:, 0:1024],
                            AF.Identity, bias=T["ld0sb"][:, 6:7]),
                          sem="gs", key=(nm + "ev", gidx, r), waits=wse)
                for r in range(2):
                    def gmm(r=r):
                        ins = None
                        for dj in range(4):
                            dd = r * 4 + dj
                            for k in range(KC):
                                ins = e.matmul(
                                    yps[:, 1024 + dj * 256:1024 + (dj + 1) * 256],
                                    T["w1sb"][:, 24576 + k * 1024 + dd * 128:
                                              24576 + k * 1024 + (dd + 1) * 128],
                                    u_ap(k), start=(k == 0), stop=(k == KC - 1))
                        return ins
                    ws = [("gp", M(("Au",) + key + (KC - 1,)))]
                    if r == 0:
                        ws.append(("gs", M(("vev", gidx - 1, 1))))
                    else:
                        ws.append(("gs", M(("gex", gidx, 0))))
                    I("t", gmm, sem="gt", key=("gmm", gidx, r), waits=ws)
                    ws2 = [("gt", M(("gmm", gidx, r))),
                           ("gt", M(("Ast",) + key)),
                           ("gs", M(("mk5",))),
                           ("gv", M(("gah", gidx, r - 1)))]
                    I("s", lambda: e.activation(T["g1h"][:, :], yps[:, 1024:2048],
                                                AF.Exp, scale=-1.0,
                                                bias=T["ld0sb"][:, 6:7]),
                      sem="gs", key=("gex", gidx, r), waits=ws2)
                    I("v", lambda: e.tensor_scalar(T["gahb"][:, :], T["g1h"][:, :],
                                                   1.0, None, ALU.add),
                      sem="gv", key=("gah", gidx, r),
                      waits=[("gs", M(("gex", gidx, r))),
                             ("gv", M(("Axb",) + key))])
                    wgr = [("gv", M(("gah", gidx, r)))]
                    if gidx == 0 and r == 0:
                        wgr.append(("gt", M(("Alt", 1, 1, 7))))
                    def grcf(r=r):
                        with nc.allow_low_precision(reason="bf16 gate operand"):
                            return e.reciprocal(
                                T["g_sb"][:, r * 1024:(r + 1) * 1024],
                                T["gahb"][:, :])
                    I("v", grcf, sem="gv", key=("grc", gidx, r), waits=wgr)
                for t in range(2):
                    def vmm(t=t):
                        ins = None
                        for c in range(2):
                            for k in range(KC):
                                ins = e.matmul(
                                    yps[:, 1024 + c * 512:1024 + (c + 1) * 512],
                                    u_tile(k, t),
                                    T["w1sb"][:, 16384 + k * 1024 + c * 512:
                                              16384 + k * 1024 + (c + 1) * 512],
                                    start=(k == 0), stop=(k == KC - 1))
                        return ins
                    ws = [("gs", M(("gex", gidx, 1)) if t == 0 else
                           M(("vev", gidx, 0)))]
                    I("t", vmm, sem="gt", key=("vmm", gidx, t), waits=ws)
                    wse = [("gt", M(("vmm", gidx, t))),
                           ("gt", M(("omm", gidx, t - 1)) if t else
                            M(("omm", gidx - 1, 1)))]
                    if gidx == 0 and t == 0:
                        wse.append(("mbcmm", p2mb))
                    I("s", lambda: e.activation(T["v_sb"][:, :], yps[:, 1024:2048],
                                                AF.Identity,
                                                bias=T["ld0sb"][:, 6:7]),
                      sem="gs", key=("vev", gidx, t), waits=wse)

                    def stmm(t=t):
                        ins = None
                        for c in range(KC):
                            ins = e.matmul(
                                mbcps[:, t * 128:(t + 1) * 128],
                                T["k_sb"][:, c * 256 + t * 128:
                                          c * 256 + t * 128 + 128],
                                T["q_sb"][:, c * 256 + t * 128:
                                          c * 256 + t * 128 + 128],
                                start=(c == 0), stop=(c == KC - 1))
                        return ins
                    ws = [("gs", M(("qev", gidx, 1))), ("gs", M(("kev", gidx, 1))),
                          ("gv", M(("smul", gidx, 0)) if t else
                           M(("smul", gidx - 1, 1)))]
                    if gidx == 0 and t == 0:
                        ws.append(("mbce", cnt["mbce"]))
                    I("t", stmm, sem="gt", key=("stmm", gidx, t), waits=ws)
                    wsm = [("gt", M(("stmm", gidx, t))), ("gv", M(("mk6",)))]
                    if gidx == 0 and t == 0:
                        wsm.append(("gt", M(("Alt", 1, 1, 7))))
                    I("v", lambda t=t: e.tensor_tensor(
                        T["s_sb"][:, t * 128:(t + 1) * 128],
                        mbcps[:, t * 128:(t + 1) * 128], T["masksb"][:, :],
                        ALU.mult),
                      sem="gv", key=("smul", gidx, t), waits=wsm)

                    def omm(t=t):
                        ins = None
                        for dd in range(KC):
                            ins = e.matmul(
                                yps[:, dd * 128:(dd + 1) * 128],
                                T["v_sb"][:, dd * 128:(dd + 1) * 128],
                                T["s_sb"][:, t * 128:(t + 1) * 128],
                                start=True, stop=True)
                        return ins
                    ws = [("gs", M(("vev", gidx, t))), ("gv", M(("smul", gidx, t))),
                          ("gv", M(("zz", gidx, 0)) if t else
                           M(("zz", gidx - 1, 1)))]
                    I("t", omm, sem="gt", key=("omm", gidx, t), waits=ws)
                    wz = [("gt", M(("omm", gidx, t))), ("gv", M(("grc", gidx, 1))),
                          ("gv", M(("gah", gidx, 1)))]
                    if gidx == 0 and t == 0:
                        wz.append(("gt", M(("Ast",) + key)))
                    I("v", lambda t=t: e.tensor_tensor(
                        sl3(T["z_sb"][:, :], 8, t * 128, t * 128 + 128),
                        sl3(yps[:, 0:1024], 8, 0, 128),
                        sl3(T["g_sb"][:, :], 8, t * 128, t * 128 + 128),
                        ALU.mult),
                      sem="gv", key=("zz", gidx, t), waits=wz)
                u3 = sl3(T["usb"][:, :], 8, uslot * 256, uslot * 256 + 256)
                xs3 = sl3(xs, 8, 0, 256)
                I("v", lambda: e.tensor_tensor(xs3, xs3, u3, ALU.add),
                  sem="gv", key=("uadd", gidx),
                  waits=[("xld", M(("xl",) + key)),
                         ("gp", M(("Au",) + key + (KC - 1,)))])
                for r in range(2):
                    def aomm(r=r):
                        ins = None
                        ps = hps if r == 0 else yps
                        for dj in range(4):
                            dd = r * 4 + dj
                            for k in range(KC):
                                ins = e.matmul(
                                    ps[:, dj * 256:(dj + 1) * 256],
                                    T["w2sb"][:, k * 1024 + dd * 128:
                                              k * 1024 + (dd + 1) * 128],
                                    T["z_sb"][:, k * 256:(k + 1) * 256],
                                    start=(k == 0), stop=(k == KC - 1))
                        return ins
                    ws = [("gv", M(("zz", gidx, 1)))]
                    if r == 0:
                        ws.append(("gs", M(("qev", gidx, 1))))
                        if gidx == 0:
                            ws.append(("wld", M(("aw", "woT"))))
                    else:
                        ws.append(("gv", M(("smul", gidx, 1))))
                    I("t", aomm, sem="gt", key=("aomm", gidx, r), waits=ws)
                    I("v", lambda r=r: e.tensor_tensor(
                        xs[:, r * 1024:(r + 1) * 1024],
                        xs[:, r * 1024:(r + 1) * 1024],
                        (hps if r == 0 else yps)[:, 0:1024], ALU.add),
                      sem="gv", key=("aoadd", gidx, r),
                      waits=[("gt", M(("aomm", gidx, r))),
                             ("gv", M(("uadd", gidx)))])
                mark[("b2e", gidx)] = M(("aoadd", gidx, 1))
                if DBG2 and gidx == 0:
                    for di, buf in enumerate(("q_sb", "k_sb", "v_sb", "s_sb",
                                              "g_sb", "z_sb")):
                        I("q", lambda di=di, buf=buf: e.dma_start(
                            out=dbg2[di, :, 0:T[buf].shape[1]], in_=T[buf][:, :]),
                          sem="xst", key=("dbg", di), n=16,
                          waits=[("gv", M(("b2e", gidx)))] if di == 0 else [])
                    I("q", lambda: e.dma_start(
                        out=dbg2[6, :, 0:2048],
                        in_=T["usb"][:, 0:2048]),
                      sem="xst", key=("dbg", 6), n=16)
                I("q", lambda xs=xs, gidx=gidx:
                    e.dma_start(out=x2s[gidx, :, :], in_=xs),
                  sem="xst", key=("xs2", gidx), n=16,
                  waits=[("gv", M(("b2e", gidx)))])
                xrel[xli] = ("xst", M(("xs2", gidx)))

            const_loads()
            for pid in (1, 3):
                if pid == 3:
                    pass2()
                for half in (0, 1):
                    phaseA(pid, half)
                    phaseB(pid, half)
            if DBG2:
                dumps = [(dbg2[7, :, 0:16], T["onesc"][:, :]),
                         (dbg2[7, 0:1, 16:144], T["onesr"][0:1, 0:128]),
                         (dbg2[7, :, 256:768], T["rb"][:, :]),
                         (dbg2[7, :, 768:832], T["swlsb"][:, :])]
                for di, (dst, srcap) in enumerate(dumps):
                    I("q", lambda dst=dst, srcap=srcap:
                        e.dma_start(out=dst, in_=srcap),
                      sem="xst", key=("cdump", di), n=16,
                      waits=([("s0", cnt["s0"]), ("stt", cnt["stt"])]
                             if di == 0 else []))
            I("q", lambda: e.wait_ge(SEM["xst"], cnt["xst"]) or None)

        for EN, section in (("t", block.tensor), ("s", block.scalar),
                            ("v", block.vector), ("p", block.gpsimd),
                            ("q", block.sync)):
            section(lambda eng, EN=EN: walk(EN, eng))

    return nc


_CACHE = {}


def kernel(L0_attn_x, L0_attn_W_q, L0_attn_W_k, L0_attn_W_v, L0_attn_W_g,
           L0_attn_W_o, L0_attn_log_decay, L0_ffn_switch_w, L0_ffn_switch_b,
           L0_ffn_w1, L0_ffn_b1, L0_ffn_w2, L0_ffn_b2):
    x = np.asarray(L0_attn_x, dtype=np.float32)
    B, TT, Dm = x.shape
    tok = x.reshape(B * TT, Dm)
    per = tok.shape[0] // NCORES

    w1 = np.ascontiguousarray(np.asarray(L0_ffn_w1, dtype=np.float32))
    w2 = np.ascontiguousarray(np.asarray(L0_ffn_w2, dtype=np.float32))
    b1 = np.asarray(L0_ffn_b1, dtype=np.float32)
    b2 = np.asarray(L0_ffn_b2, dtype=np.float32)
    b1lh = np.ascontiguousarray(b1.reshape(NE, FT, 128).transpose(0, 2, 1))
    b2lh = np.ascontiguousarray(b2.reshape(NE, KC, 128).transpose(0, 2, 1))
    sw = np.asarray(L0_ffn_switch_w, dtype=np.float32)
    swl = np.ascontiguousarray(
        sw.reshape(KC, 128, 8).transpose(1, 0, 2)).reshape(128, 64)
    swb = np.ascontiguousarray(
        np.asarray(L0_ffn_switch_b, dtype=np.float32).reshape(8, 1))
    aw = {}
    for nm, W in (("wqT", L0_attn_W_q), ("wkT", L0_attn_W_k),
                  ("wvT", L0_attn_W_v), ("wgT", L0_attn_W_g),
                  ("woT", L0_attn_W_o)):
        aw[nm] = np.ascontiguousarray(np.asarray(W, dtype=np.float32).T)
    ld0 = np.full((128, 1), np.asarray(L0_attn_log_decay,
                                       dtype=np.float32)[0], dtype=np.float32)
    onesc = np.ones((128, 1), dtype=ml_dtypes.bfloat16)
    onesr = np.zeros((128, 128), dtype=ml_dtypes.bfloat16)
    onesr[0, :] = 1.0
    selc = np.zeros((128, 1024), dtype=ml_dtypes.bfloat16)
    for ex in range(8):
        selc[ex, ex * 128:(ex + 1) * 128] = 1.0
    identf = np.eye(128, dtype=np.float32)
    ii, jj = np.meshgrid(np.arange(128), np.arange(128), indexing="ij")
    same = (ii // 32) == (jj // 32)
    ge = jj >= ii
    diffc = np.where(same & ge, (jj - ii).astype(np.float32), 0.0
                     ).astype(np.float32)
    indc = (same & ge).astype(np.float32)

    if "nc" not in _CACHE:
        _CACHE["nc"] = build_program()
    nc = _CACHE["nc"]
    in_maps = []
    for c in range(NCORES):
        sl = tok[c * per:(c + 1) * per, :]
        fm = np.ascontiguousarray(sl.T)
        xt = np.ascontiguousarray(
            fm.reshape(KC, 128, 16, TG).transpose(2, 1, 0, 3)
        ).reshape(16, 128, KC * TG)
        in_maps.append({
            "xin": xt, "w1": w1, "w2": w2, "b1l": b1lh, "b2l": b2lh,
            "swli": swl, "swbi": swb,
            "wqT": aw["wqT"], "wkT": aw["wkT"], "wvT": aw["wvT"],
            "wgT": aw["wgT"], "woT": aw["woT"], "ld0i": ld0,
            "onesci": onesc, "onesri": onesr, "seli": selc,
            "identfi": identf, "diffi": diffc, "indi": indc,
        })
    res = run_bass_kernel_spmd(nc, in_maps, core_ids=list(range(NCORES)))
    outs = []
    for c in range(NCORES):
        o = np.asarray(res.results[c]["out"])
        fm = o.reshape(16, 128, KC, TG).transpose(2, 1, 0, 3).reshape(D, per)
        outs.append(np.ascontiguousarray(fm.T))
    full = np.concatenate(outs, axis=0).reshape(B, TT, Dm)
    return full.astype(np.float32)


# version 09
# speedup vs baseline: 1.0373x; 1.0373x over previous
import numpy as np
import ml_dtypes
import concourse.bass as bass
import concourse.mybir as mybir
from concourse.bass_utils import run_bass_kernel_spmd

F32 = mybir.dt.float32
BF16 = mybir.dt.bfloat16
AF = mybir.ActivationFunctionType
ALU = mybir.AluOpType
AXX = mybir.AxisListType.X

NCORES = 8
D = 1024
FF = 4096
KC = 8
FT = 32
TG = 256
NST = 16
NE = 8
YLAG = 3
HP = 5
EPS = 1e-5
DBG2 = False


def _sbmap():
    m = {}
    off = [16384]
    def add(name, nbytes, dt):
        m[name] = (off[0], nbytes, dt)
        off[0] += nbytes
    add("w1sb", 65536, BF16)
    add("w2sb", 65536, BF16)
    add("usb", 32768, BF16)
    add("xsb", 16384, F32)
    add("hbf", 5120, BF16)
    add("tmp", 2048, F32)
    add("mbcs", 2048, F32)
    add("mfm", 4096, BF16)
    add("sq16", 4096, BF16)
    add("bcsb", 2048, F32)
    add("rows", 4096, F32)
    add("rb", 1024, BF16)
    add("lgsb", 1024, F32)
    add("sel", 2048, BF16)
    add("onesc", 32, BF16)
    add("onesr", 256, BF16)
    add("identf", 512, F32)
    add("diffc", 512, F32)
    add("indc", 512, F32)
    add("masksb", 512, F32)
    add("ld0sb", 32, F32)
    add("rsm", 512, F32)
    add("b1sb", 256, F32)
    add("b2sb", 64, F32)
    add("swlsb", 128, BF16)
    add("swbsb", 32, F32)
    assert off[0] <= 228500, off[0]
    m["xb16b"] = (m["tmp"][0], 4096, BF16)
    m["gahb"] = (m["hbf"][0], 4096, F32)
    m["q_sb"] = (m["tmp"][0], 4096, BF16)
    m["k_sb"] = (m["mfm"][0], 4096, BF16)
    m["g1h"] = (m["sq16"][0], 4096, F32)
    m["z_sb"] = (m["sq16"][0], 4096, BF16)
    m["g_sb"] = (m["bcsb"][0], 4096, BF16)
    m["v_sb"] = (m["sel"][0], 2048, BF16)
    m["s_sb"] = (m["lgsb"][0] + 512, 512, BF16)
    return m


def sl3(ap2d, c, lo, hi):
    return ap2d.rearrange("p (c w) -> p c w", c=c)[:, :, lo:hi]


def build_program():
    nc = bass.Bass("TRN2", target_bir_lowering=False)

    xin = nc.dram_tensor("xin", [16, 128, KC * TG], F32, kind="ExternalInput").ap()
    w1 = nc.dram_tensor("w1", [NE, D, FF], F32, kind="ExternalInput").ap()
    w2 = nc.dram_tensor("w2", [NE, FF, D], F32, kind="ExternalInput").ap()
    b1l = nc.dram_tensor("b1l", [NE, 128, FT], F32, kind="ExternalInput").ap()
    b2l = nc.dram_tensor("b2l", [NE, 128, KC], F32, kind="ExternalInput").ap()
    swl_in = nc.dram_tensor("swli", [128, 64], F32, kind="ExternalInput").ap()
    swb_in = nc.dram_tensor("swbi", [8, 1], F32, kind="ExternalInput").ap()
    awT = {n: nc.dram_tensor(n, [D, D], F32, kind="ExternalInput").ap()
           for n in ("wqT", "wkT", "wvT", "wgT", "woT")}
    ld0_in = nc.dram_tensor("ld0i", [128, 1], F32, kind="ExternalInput").ap()
    onesc_in = nc.dram_tensor("onesci", [128, 1], BF16, kind="ExternalInput").ap()
    onesr_in = nc.dram_tensor("onesri", [128, 128], BF16, kind="ExternalInput").ap()
    sel_in = nc.dram_tensor("seli", [128, 1024], BF16, kind="ExternalInput").ap()
    identf_in = nc.dram_tensor("identfi", [128, 128], F32, kind="ExternalInput").ap()
    diff_in = nc.dram_tensor("diffi", [128, 128], F32, kind="ExternalInput").ap()
    ind_in = nc.dram_tensor("indi", [128, 128], F32, kind="ExternalInput").ap()

    x1s = nc.dram_tensor("x1s", [16, 128, KC * TG], F32, kind="Internal").ap()
    x2s = nc.dram_tensor("x2s", [16, 128, KC * TG], F32, kind="Internal").ap()
    x3s = nc.dram_tensor("x3s", [16, 128, KC * TG], F32, kind="Internal").ap()
    out = nc.dram_tensor("out", [16, 128, KC * TG], F32, kind="ExternalOutput").ap()
    dbg2 = (nc.dram_tensor("dbg2", [10, 128, 2048], BF16,
                           kind="ExternalOutput").ap() if DBG2 else None)
    dbgu = (nc.dram_tensor("dbgu", [128, 16384], BF16,
                           kind="ExternalOutput").ap() if DBG2 else None)
    dbgr = (nc.dram_tensor("dbgr", [4, 128, 1024], F32,
                           kind="ExternalOutput").ap() if DBG2 else None)

    SBM = _sbmap()
    from contextlib import ExitStack
    with ExitStack() as stk:
        E = stk.enter_context
        T = {}
        for name, (off, nbytes, dt) in SBM.items():
            esz = 2 if dt == BF16 else 4
            T[name] = nc.alloc_sbuf_tensor_at(name, [128, nbytes // esz], dt,
                                              offset=off)
        hps = E(nc.psum_tensor("hps", [128, 1024], F32))
        yps = E(nc.psum_tensor("yps", [128, 2048], F32))
        mbcps = E(nc.psum_tensor("mbcps", [128, 512], F32))
        rops = E(nc.psum_tensor("rops", [128, 512], F32))

        sem_names = ["wld", "cld", "bld", "xld", "xle", "xst", "xse",
                     "hmm", "s0", "ymm", "stt", "pad", "mbcmm", "mbce", "gt",
                     "gs", "gv", "gp"]
        SEM = {n: E(nc.semaphore(n)) for n in sem_names}
        block = E(nc.Block())

        def walk(EN, e):
            cnt = {n: 0 for n in sem_names}
            mark = {}
            xrel = {}
            S8 = {"xli": 0, "ei": 0, "ust": 0, "yinfo": {}, "lastustt": 0,
                  "lastpad": 0, "prevunit": None, "uq": 0}

            def I(engine, f, sem=None, key=None, n=1, waits=()):
                if EN == engine:
                    for s, v in waits:
                        if v:
                            e.wait_ge(SEM[s], v)
                    if n == 16 and sem is not None and engine in ("q", "s") \
                            and cnt[sem]:
                        e.wait_ge(SEM[sem], cnt[sem])
                    ins = f()
                    if sem is not None and ins is not None:
                        ins.then_inc(SEM[sem], n)
                if sem is not None:
                    cnt[sem] += n
                    if key is not None:
                        mark[key] = cnt[sem]

            def M(key):
                return mark.get(key, 0)

            def XL(xli):
                return ("xld", "xle")[xli % 2]

            def XS(xli):
                return ("xst", "xse")[xli % 2]

            def const_loads():
                pairs = [(T["swbsb"][0:8, 0:1], swb_in[:, :]),
                         (T["onesc"][:, 0:1], onesc_in[:, :]),
                         (T["onesr"][:, :], onesr_in[:, :]),
                         (T["sel"][:, :], sel_in[:, :]),
                         (T["identf"][:, :], identf_in[:, :]),
                         (T["diffc"][:, :], diff_in[:, :]),
                         (T["indc"][:, :], ind_in[:, :]),
                         (T["ld0sb"][:, 0:1], ld0_in[:, :])]
                for dst, src in pairs:
                    I("q", lambda dst=dst, src=src: e.dma_start(out=dst, in_=src),
                      sem="cld", n=16)
                mark[("cld0",)] = cnt["cld"]
                I("p", lambda: e.dma_start(out=T["swlsb"][:, :], in_=swl_in[:, :]),
                  sem="wld", key=("swl",), n=16)
                I("p", lambda: e.memset(T["ld0sb"][:, 5:6], EPS),
                  sem="gp", key=("eps",))
                I("p", lambda: e.memset(T["ld0sb"][:, 6:7], 0.0),
                  sem="gp", key=("zb0",))
                I("p", lambda: e.memset(T["rb"][:, :], 0.0),
                  sem="gp", key=("rb0",))
                I("p", lambda: e.memset(T["mfm"][:, :], 0.0),
                  sem="gp", key=("mfm0",))

            def ln_block(pid, half, g, xs, xb, key, uslot, xsem="xld",
                         extraT=(), extraV=(), extraS=(), extraP=()):
                I("v", lambda: e.tensor_tensor(T["sq16"][:, :], xs, xs, ALU.mult),
                  sem="gv", key=("Asq",) + key,
                  waits=[(xsem, M(("xl",) + key)),
                         ("gt", M(("Ast",) + key[:-1] + (key[-1] - 1,)))]
                  + list(extraS))

                def stats():
                    for k in range(KC):
                        e.matmul(rops[0:1, 0:256], T["onesc"][:, 0:1],
                                 xb[:, k * 256:(k + 1) * 256],
                                 start=(k == 0), stop=(k == KC - 1))
                    ins = None
                    for k in range(KC):
                        ins = e.matmul(rops[0:1, 256:512], T["onesc"][:, 0:1],
                                       T["sq16"][:, k * 256:(k + 1) * 256],
                                       start=(k == 0), stop=(k == KC - 1))
                    return ins
                I("t", stats, sem="gt", key=("Ast",) + key,
                  waits=[("gv", M(("Axb",) + key)), ("gv", M(("Asq",) + key)),
                         ("gs", M(("Abce",) + key[:-1] + (key[-1] - 1,))),
                         ("cld", M(("cld0",)))] + list(extraT))

                rA = T["rows"][0:1, 0:256]
                rB = T["rows"][0:1, 256:512]
                rC = T["rows"][0:1, 512:768]
                rD = T["rows"][0:1, 768:1024]

                def rowse():
                    e.activation(rA, rops[0:1, 0:256], AF.Identity,
                                 bias=T["ld0sb"][0:1, 6:7])
                    return e.activation(rB, rops[0:1, 256:512], AF.Identity,
                                        bias=T["ld0sb"][0:1, 6:7])
                I("s", rowse, sem="gs", key=("Arw",) + key,
                  waits=[("gt", M(("Ast",) + key)),
                         ("gt", M(("Abc",) + key[:-1] + (key[-1] - 1,)))])
                I("v", lambda: e.scalar_tensor_tensor(
                    rC, rA, 1.0 / 1024.0, rA, ALU.mult, ALU.mult),
                  sem="gv", key=("Am2",) + key,
                  waits=[("gs", M(("Arw",) + key))])
                I("p", lambda: e.tensor_tensor(rD, rB, rC, ALU.subtract),
                  sem="gp", key=("Avr",) + key,
                  waits=[("gv", M(("Am2",) + key))])
                I("s", lambda: e.activation(rC, rD, AF.Ln,
                                            bias=T["ld0sb"][0:1, 5:6],
                                            scale=1.0 / 1024.0),
                  sem="gs", key=("Aln",) + key,
                  waits=[("gp", M(("Avr",) + key)), ("gp", M(("eps",)))])
                I("s", lambda: e.activation(rB, rC, AF.Exp, scale=-0.5,
                                            bias=T["ld0sb"][0:1, 6:7]),
                  sem="gs", key=("Ars",) + key,
                  waits=[("gs", M(("Aln",) + key))])
                I("v", lambda: e.scalar_tensor_tensor(
                    rC, rA, 1.0 / 1024.0, rB, ALU.mult, ALU.mult),
                  sem="gv", key=("Amr",) + key,
                  waits=[("gs", M(("Ars",) + key))])

                def rbc():
                    e.tensor_copy(T["rb"][0:1, 0:256], rB)
                    return e.tensor_copy(T["rb"][0:1, 256:512], rC)
                I("p", rbc, sem="gp", key=("Arb",) + key,
                  waits=[("gv", M(("Amr",) + key)), ("gp", M(("rb0",)))])

                def bc():
                    e.matmul(hps[:, 0:256], T["onesr"][:, :],
                             T["rb"][:, 0:256], start=True, stop=True)
                    return e.matmul(hps[:, 256:512], T["onesr"][:, :],
                                    T["rb"][:, 256:512], start=True, stop=True)
                I("t", bc, sem="gt", key=("Abc",) + key,
                  waits=[("gp", M(("Arb",) + key)), ("gs", M(("Arw",) + key))])
                I("s", lambda: e.activation(T["bcsb"][:, :], hps[:, 0:512],
                                            AF.Identity,
                                            bias=T["ld0sb"][:, 6:7]),
                  sem="gs", key=("Abce",) + key,
                  waits=[("gt", M(("Abc",) + key)),
                         ("gp", M(("Au",) + key[:-1] + (key[-1] - 1, KC - 1))),
                         ("gp", M(("zb0",)))])
                for k in range(KC):
                    wv = [("gs", M(("Abce",) + key)),
                          ("gp", M(("Au",) + key + (k - 1,)) if k else
                           M(("Au",) + key[:-1] + (key[-1] - 1, KC - 1)))]
                    if k == 0:
                        wv += list(extraV)
                    I("v", lambda k=k: e.tensor_tensor(
                        T["tmp"][:, 0:256], xs[:, k * 256:(k + 1) * 256],
                        T["bcsb"][:, 0:256], ALU.mult),
                      sem="gv", key=("At",) + key + (k,), waits=wv)
                    wp = [("gv", M(("At",) + key + (k,)))]
                    if k == 0:
                        wp += list(extraP)
                    I("p", lambda k=k: e.tensor_tensor(
                        T["usb"][:, k * 2048 + uslot * 256:
                                 k * 2048 + uslot * 256 + 256],
                        T["tmp"][:, 0:256], T["bcsb"][:, 256:512], ALU.subtract),
                      sem="gp", key=("Au",) + key + (k,), waits=wp)

            def phaseA(pid, half):
                xsrc = xin if pid == 1 else x2s
                lasth = M(("lasth",)); lasty = M(("lasty",))
                lasts0 = M(("lasts0",))
                att = M(("attend",)); attv = M(("attendv",))
                p3b = (pid == 3 and half == 0)
                for g in range(8):
                    key = (pid, half, g)
                    gidx = half * 8 + g
                    xli = S8["xli"]; S8["xli"] += 1
                    rel = xrel.get(xli - 2)
                    xs = T["xsb"][:, (xli % 2) * 2048:(xli % 2) * 2048 + 2048]
                    I("q", lambda xs=xs, gidx=gidx, xsrc=xsrc:
                        e.dma_start(out=xs, in_=xsrc[gidx, :, :]),
                      sem=XL(xli), key=("xl",) + key, n=16,
                      waits=([rel] if rel else []))
                    xb = (T["hbf"][:, 0:2048] if xli % 2 == 0 else T["xb16b"][:, :])
                    wx = [(XL(xli), M(("xl",) + key)),
                          ("gt", M(("Ast", pid, half, g - 2)))]
                    if g < 2:
                        wx += [("ymm", lasty), ("s0", lasts0),
                               ("pad", S8["lastpad"]), ("stt", S8["lastustt"])]
                        if p3b:
                            wx += [("gt", att), ("gv", attv)]
                    I("v", lambda xb=xb, xs=xs: e.tensor_copy(xb, xs),
                      sem="gv", key=("Axb",) + key, waits=wx)
                    eT = [("gt", att)] if (p3b and g == 0) else []
                    if DBG2 and pid == 1 and half == 1 and g == 1:
                        eT = eT + [("xst", M(("dbgr3",)))]
                    if DBG2 and pid == 1 and half == 0 and g == 1:
                        eT = eT + [("xst", M(("dbgh0",)))]
                    eS = [("gt", att)] if (p3b and g == 0) else []
                    eV = ([("pad", S8["lastpad"]), ("gt", att)]
                          if g == 0 else [])
                    eP = ([("hmm", lasth), ("gt", att)] if g == 0 else [])
                    ln_block(pid, half, g, xs, xb, key, g, xsem=XL(xli),
                             extraT=eT, extraV=eV, extraS=eS, extraP=eP)
                    if DBG2 and pid == 1 and half == 0 and g == 0:
                        I("q", lambda: e.dma_start(out=dbgr[0, :, 0:512],
                                                   in_=T["bcsb"][:, :]),
                          sem="xst", key=("dbgh0",), n=16,
                          waits=[("gp", M(("Au",) + key + (KC - 1,)))])
                    if DBG2 and pid == 1 and half == 1 and g == 0:
                        I("q", lambda: e.dma_start(out=dbgr[2, :, 0:512],
                                                   in_=T["bcsb"][:, :]),
                          sem="xst", key=("dbgr2",), n=16,
                          waits=[("gp", M(("Au",) + key + (KC - 1,)))])
                        I("q", lambda: e.dma_start(out=dbgr[3, :, :],
                                                   in_=T["rows"][:, :]),
                          sem="xst", key=("dbgr3",), n=16)
                    xrel[xli] = ("gv", M(("At",) + key + (KC - 1,)))

                    def lgmm(g=g):
                        ins = None
                        for k in range(KC):
                            ins = e.matmul(hps[0:8, 0:256],
                                           T["swlsb"][:, k * 8:(k + 1) * 8],
                                           T["usb"][:, k * 2048 + g * 256:
                                                    k * 2048 + g * 256 + 256],
                                           start=(k == 0), stop=(k == KC - 1))
                        return ins
                    wl = [("gp", M(("Au",) + key + (KC - 1,))),
                          ("wld", M(("swl",)))]
                    if g == 0:
                        wl.append(("s0", lasts0))
                    I("t", lgmm, sem="gt", key=("Alg",) + key, waits=wl)
                    I("s", lambda: e.activation(T["lgsb"][0:8, 0:128],
                                                hps[0:8, 0:128], AF.Identity,
                                                bias=T["swbsb"][0:8, 0:1]),
                      sem="gs", key=("Alg1",) + key,
                      waits=[("gt", M(("Alg",) + key)),
                             ("gt", M(("Amf", pid, half, g - 1)))])
                    I("s", lambda: e.activation(T["lgsb"][0:8, 128:256],
                                                hps[0:8, 128:256], AF.Identity,
                                                bias=T["swbsb"][0:8, 0:1]),
                      sem="gs", key=("Alge",) + key, waits=[])

                    def ltt():
                        e.transpose(hps[:, 512:520], T["lgsb"][0:8, 0:128],
                                    T["identf"][0:8, 0:8])
                        return e.transpose(hps[:, 520:528], T["lgsb"][0:8, 128:256],
                                           T["identf"][0:8, 0:8])
                    I("t", ltt, sem="gt", key=("Alt",) + key,
                      waits=[("gs", M(("Alge",) + key))])

                    def nmx():
                        e.reduce_max(T["rsm"][:, 0:1], hps[:, 512:520], axis=AXX,
                                     negate=True)
                        return e.reduce_max(T["rsm"][:, 1:2], hps[:, 520:528],
                                            axis=AXX, negate=True)
                    I("v", nmx, sem="gv", key=("Anx",) + key,
                      waits=[("gt", M(("Alt",) + key)),
                             ("gv", M(("Aid", pid, half, g - 1)))])

                    def exs():
                        e.activation(T["rsm"][:, 8:16], hps[:, 512:520], AF.Exp,
                                     bias=T["rsm"][:, 0:1])
                        e.activation(T["rsm"][:, 16:24], hps[:, 520:528], AF.Exp,
                                     bias=T["rsm"][:, 1:2])
                        e.activation(T["rsm"][:, 24:32], hps[:, 512:520],
                                     AF.Identity, bias=T["ld0sb"][:, 6:7])
                        return e.activation(T["rsm"][:, 32:40], hps[:, 520:528],
                                            AF.Identity,
                                            bias=T["ld0sb"][:, 6:7])
                    I("s", exs, sem="gs", key=("Aex",) + key,
                      waits=[("gv", M(("Anx",) + key)),
                             ("gv", M(("Ase", pid, half, g - 1))),
                             ("gv", M(("Aid", pid, half, g - 1)))])

                    def ses():
                        e.reduce_sum(T["rsm"][:, 2:3], T["rsm"][:, 8:16], axis=AXX)
                        return e.reduce_sum(T["rsm"][:, 3:4], T["rsm"][:, 16:24],
                                            axis=AXX)
                    I("v", ses, sem="gv", key=("Ase",) + key,
                      waits=[("gs", M(("Aex",) + key))])

                    def rcs():
                        e.reciprocal(T["rsm"][:, 4:5], T["rsm"][:, 2:3])
                        return e.reciprocal(T["rsm"][:, 5:6], T["rsm"][:, 3:4])
                    I("v", rcs, sem="gv", key=("Arc",) + key,
                      waits=[("gv", M(("Ase",) + key))])

                    def inds():
                        e.tensor_scalar(T["rsm"][:, 40:48], T["rsm"][:, 24:32],
                                        T["rsm"][:, 0:1], 0.0, ALU.add,
                                        ALU.is_equal)
                        return e.tensor_scalar(T["rsm"][:, 48:56],
                                               T["rsm"][:, 32:40],
                                               T["rsm"][:, 1:2], 0.0, ALU.add,
                                               ALU.is_equal)
                    I("v", inds, sem="gv", key=("Aid",) + key,
                      waits=[("gs", M(("Aex",) + key))])

                    def mtoks():
                        e.tensor_scalar(T["rsm"][:, 56:64], T["rsm"][:, 40:48],
                                        T["rsm"][:, 4:5], 0.5, ALU.mult, ALU.mult)
                        return e.tensor_scalar(T["rsm"][:, 64:72],
                                               T["rsm"][:, 48:56],
                                               T["rsm"][:, 5:6], 0.5, ALU.mult,
                                               ALU.mult)
                    I("v", mtoks, sem="gv", key=("Amt",) + key,
                      waits=[("gv", M(("Aid",) + key)),
                             ("gv", M(("Arc",) + key)),
                             ("gt", M(("Amf", pid, half, g - 1)))])

                    def mft():
                        e.transpose(yps[0:8, 0:128], T["rsm"][:, 56:64],
                                    T["identf"][:, :])
                        return e.transpose(yps[0:8, 512:640], T["rsm"][:, 64:72],
                                           T["identf"][:, :])
                    wm = [("gv", M(("Amt",) + key))]
                    if g == 0:
                        wm += [("ymm", lasty), ("stt", S8["lastustt"])]
                        if p3b:
                            wm.append(("gv", attv))
                    I("t", mft, sem="gt", key=("Amf",) + key, waits=wm)

                    def mfe(g=g):
                        e.activation(T["mfm"][0:8, g * 256:g * 256 + 128],
                                     yps[0:8, 0:128], AF.Identity,
                                     bias=T["ld0sb"][0:8, 6:7])
                        return e.activation(T["mfm"][0:8, g * 256 + 128:
                                                     g * 256 + 256],
                                            yps[0:8, 512:640], AF.Identity,
                                            bias=T["ld0sb"][0:8, 6:7])
                    we = [("gt", M(("Amf",) + key))]
                    if g == 0:
                        we.append(("mbcmm", cnt["mbcmm"]))
                        we.append(("gp", M(("mfmr", pid)) or M(("mfm0",))))
                    I("s", mfe, sem="gs", key=("Ame",) + key, waits=we)

            def phaseB(pid, half):
                eorder = (list(range(NE)) if half == 0
                          else list(range(NE - 1, -1, -1)))
                src_first = xin if pid == 1 else x2s
                scr = x1s if pid == 1 else x3s
                NLOC = NE * 8 * NST
                base_u = S8["ust"]
                pB_lastustt = S8["lastustt"]

                def expert_pre(en, ex):
                    ei = S8["ei"]
                    mark[("uend", ei)] = S8["ust"]
                    prev = S8["ust"]
                    for kb in range(KC):
                        wv = []
                        if kb == 0 and ei > 0:
                            wv.append(("hmm", prev))
                            if pid == 3 and en == 0 and half == 0:
                                wv.append(("gt", M(("attend",))))
                        I("p", lambda ex=ex, kb=kb: e.dma_start(
                            out=T["w1sb"][:, kb * FF:(kb + 1) * FF],
                            in_=w1[ex, kb * 128:(kb + 1) * 128, :]),
                          sem="wld", key=("w1", ei, kb), n=16, waits=wv)
                    mark[("w1f", ei)] = cnt["wld"]
                    for fb in range(4):
                        wv = []
                        if fb == 0 and ei > 0:
                            wv.append(("ymm", prev))
                            if pid == 3 and en == 0 and half == 0:
                                wv.append(("gt", M(("attend",))))
                        I("p", lambda ex=ex, fb=fb: e.dma_start(
                            out=sl3(T["w2sb"][:, fb * 8 * D:(fb + 1) * 8 * D], 8,
                                    0, D),
                            in_=w2[ex, fb * 1024:(fb + 1) * 1024, :].rearrange(
                                "(c p) d -> p c d", p=128)),
                          sem="wld", key=("w2", ei, fb), n=16, waits=wv)
                    mark[("w2f", ei)] = cnt["wld"]
                    bpar = ei % 2
                    I("q", lambda ex=ex, bpar=bpar: e.dma_start(
                        out=T["b1sb"][:, bpar * 32:(bpar + 1) * 32],
                        in_=b1l[ex, :, :]),
                      sem="bld", key=("b1", ei), n=16,
                      waits=[("s0", M(("uend", ei - 1)))])
                    I("q", lambda ex=ex, bpar=bpar: e.dma_start(
                        out=T["b2sb"][:, bpar * 8:(bpar + 1) * 8],
                        in_=b2l[ex, :, :]),
                      sem="bld", key=("b2", ei), n=16,
                      waits=[("stt", max(0, 64 * (ei - 1)))])

                def grp_pre(en, ex, g):
                    if g == 0:
                        S8["ei"] += 1
                    ei = S8["ei"] - 1
                    unit = (pid, half, en, g)
                    gidx = half * 8 + g
                    xli = S8["xli"]; S8["xli"] += 1
                    mark[("uxli", unit)] = xli
                    rel = xrel.get(xli - 2)
                    xs = T["xsb"][:, (xli % 2) * 2048:(xli % 2) * 2048 + 2048]
                    src = src_first if en == 0 else scr
                    I("q", lambda xs=xs, src=src, gidx=gidx:
                        e.dma_start(out=xs, in_=src[gidx, :, :]),
                      sem=XL(xli), key=("xl",) + unit, n=16,
                      waits=([rel] if rel else []))
                    wmb = [("mbce", cnt["mbce"]),
                           ("gs", M(("Ame", pid, half, g)))]
                    if pid == 3 and en == 0 and g == 0 and half == 0:
                        wmb.append(("gv", M(("attendv",))))
                    if pid == 3:
                        wmb.append(("cld", M(("selr",))))
                    I("t", lambda ex=ex, g=g: e.matmul(
                        mbcps[:, 0:256], T["sel"][:, ex * 128:(ex + 1) * 128],
                        T["mfm"][:, g * 256:(g + 1) * 256], start=True,
                        stop=True),
                      sem="mbcmm", key=("mbcm",) + unit, waits=wmb)
                    uq = S8["uq"]; S8["uq"] += 1
                    wme = [("mbcmm", M(("mbcm",) + unit)),
                           ("stt", max(0, 8 * (uq - 1)))]
                    if en == 0 and g == 0:
                        wme.append(("gt", M(("Ast", pid, half, 7))))
                    I("s", lambda xli=xli: e.activation(
                        T["mbcs"][:, (xli % 2) * 256:(xli % 2) * 256 + 256],
                        mbcps[:, 0:256], AF.Identity,
                        bias=T["ld0sb"][:, 6:7]),
                      sem="mbce", key=("mbce",) + unit, waits=wme)
                    return unit, xli, xs, ei

                def emit_y(yu):
                    (en, ex, g, fg, ei, bpar, xli, unit) = S8["yinfo"][yu]

                    def ymm():
                        ins = None
                        for i in range(2):
                            ft = 2 * fg + i
                            for d in range(KC):
                                ins = e.matmul(
                                    yps[:, d * 256:(d + 1) * 256],
                                    T["w2sb"][:, ft * D + d * 128:
                                              ft * D + (d + 1) * 128],
                                    T["hbf"][:, (yu % HP) * 512 + i * 256:
                                             (yu % HP) * 512 + (i + 1) * 256],
                                    start=(ft == 0 and d % 2 == 0),
                                    stop=(ft == FT - 1 and d % 2 == 1))
                        return ins
                    yw = [("s0", M(("s0", yu)))]
                    if fg == 0:
                        yw.append(("stt", S8["lastustt"]))
                        if g == 0:
                            yw.append(("wld", M(("w2f", ei))))
                    I("t", ymm, sem="ymm", key=("y", yu), waits=yw)

                def emit_evac(yu_last):
                    (en, ex, g, fg, ei, bpar, xli, unit) = S8["yinfo"][yu_last]
                    gidx = half * 8 + g
                    xs = T["xsb"][:, (xli % 2) * 2048:(xli % 2) * 2048 + 2048]
                    pu = S8["prevunit"]
                    for d in range(KC):
                        ws = [("pad", M(("pad",) + unit + (d - 2,)))]
                        if d == 0:
                            ws += [("ymm", M(("y", yu_last))),
                                   ("mbce", M(("mbce",) + unit)),
                                   (XL(xli), M(("xl",) + unit)),
                                   ("pad", M(("pad",) + (pu or ("-",)) + (6,)))]
                            if en == 0 and g == 0:
                                ws.append(("gp", M(("Au", pid, half, 7, KC - 1))))
                                if pid == 3 and half == 0:
                                    ws.append(("gt", M(("attend",))))
                        if d == 1:
                            ws.append(("pad", M(("pad",) + (pu or ("-",)) + (7,))))
                        I("v", lambda d=d, bpar=bpar, xli=xli, xs=xs:
                            e.scalar_tensor_tensor(
                                T["tmp"][:, (d % 2) * 256:(d % 2) * 256 + 256],
                                yps[:, d * 256:(d + 1) * 256],
                                T["b2sb"][:, bpar * 8 + d: bpar * 8 + d + 1],
                                T["mbcs"][:, (xli % 2) * 256:
                                          (xli % 2) * 256 + 256],
                                ALU.add, ALU.mult),
                          sem="stt", key=("stt",) + unit + (d,), waits=ws)
                        I("p", lambda d=d, xs=xs: e.tensor_tensor(
                            xs[:, d * 256:(d + 1) * 256],
                            xs[:, d * 256:(d + 1) * 256],
                            T["tmp"][:, (d % 2) * 256:(d % 2) * 256 + 256],
                            ALU.add),
                          sem="pad", key=("pad",) + unit + (d,),
                          waits=[("stt", M(("stt",) + unit + (d,)))])
                    mark[("ustt", xli)] = M(("stt",) + unit + (KC - 1,))
                    S8["lastustt"] = M(("stt",) + unit + (KC - 1,))
                    S8["lastpad"] = M(("pad",) + unit + (KC - 1,))
                    S8["prevunit"] = unit
                    dst = out if (pid == 3 and en == NE - 1) else scr
                    I("q", lambda dst=dst, gidx=gidx, xs=xs:
                        e.dma_start(out=dst[gidx, :, :], in_=xs),
                      sem=XS(xli), key=("xs",) + unit, n=16,
                      waits=[("pad", M(("pad",) + unit + (KC - 1,)))])
                    xrel[xli] = (XS(xli), M(("xs",) + unit))

                cur = {}
                for ls in range(NLOC + YLAG + 1):
                    if ls < NLOC:
                        en = ls // 128
                        g = (ls // NST) % 8
                        fg = ls % NST
                        ex = eorder[en]
                        if fg == 0 and g == 0:
                            expert_pre(en, ex)
                        if fg == 0:
                            cur["u"] = grp_pre(en, ex, g)
                        unit, xli, xs, ei = cur["u"]
                        bpar = ei % 2
                        u = S8["ust"]

                        def hmm(g=g, fg=fg, u=u):
                            ins = None
                            for i in range(2):
                                ft = 2 * fg + i
                                for k in range(KC):
                                    ins = e.matmul(
                                        hps[:, (u % 2) * 512 + i * 256:
                                            (u % 2) * 512 + (i + 1) * 256],
                                        T["w1sb"][:, k * FF + ft * 128:
                                                  k * FF + (ft + 1) * 128],
                                        T["usb"][:, k * 2048 + g * 256:
                                                 k * 2048 + g * 256 + 256],
                                        start=(k == 0), stop=(k == KC - 1))
                            return ins
                        hw = [("s0", M(("s0", u - 2)))]
                        if fg == 0 and g == 0:
                            hw.append(("wld", M(("w1f", ei))))
                            hw.append(("gp", M(("Au", pid, half, 7, KC - 1))))
                        I("t", hmm, sem="hmm", key=("h", u), waits=hw)

                        def gel(fg=fg, bpar=bpar, u=u):
                            ins = None
                            for i in range(2):
                                ft = 2 * fg + i
                                ins = e.activation(
                                    T["hbf"][:, (u % HP) * 512 + i * 256:
                                             (u % HP) * 512 + (i + 1) * 256],
                                    hps[:, (u % 2) * 512 + i * 256:
                                        (u % 2) * 512 + (i + 1) * 256],
                                    AF.Gelu_apprx_tanh,
                                    bias=T["b1sb"][:, bpar * 32 + ft:
                                                   bpar * 32 + ft + 1])
                            return ins
                        gw = [("hmm", M(("h", u))), ("ymm", M(("y", u - HP + 1)))]
                        if fg == 0 and g == 0:
                            gw.append(("bld", M(("b1", ei))))
                            if en == 0:
                                gw.append(("gt", M(("Ast", pid, half, 7))))
                        I("s", gel, sem="s0", key=("s0", u), waits=gw)
                        S8["yinfo"][u] = (en, ex, g, fg, ei, bpar, xli, unit)
                        S8["ust"] += 1
                    yls = ls - YLAG
                    if 0 <= yls < NLOC:
                        yu = base_u + yls
                        emit_y(yu)
                        if yls % NST == NST - 1:
                            emit_evac(yu)
                mark[("lasth",)] = cnt["hmm"]
                mark[("lasty",)] = cnt["ymm"]
                mark[("lasts0",)] = cnt["s0"]

            def pass2():
                lh = M(("lasth",)); ly = M(("lasty",)); ls0 = M(("lasts0",))
                lstt = S8["lastustt"]; lpad = S8["lastpad"]
                p2mb = cnt["mbcmm"]
                if DBG2:
                    I("q", lambda: e.dma_start(out=dbgu[:, :], in_=T["usb"][:, :]),
                      sem="xst", key=("dbgu",), n=16,
                      waits=[("ymm", ly)])
                aws = [("wqT", "w1sb", 0), ("wkT", "w1sb", 8192),
                       ("wvT", "w1sb", 16384), ("wgT", "w1sb", 24576),
                       ("woT", "w2sb", 0)]
                for i, (nm, dstb, woff) in enumerate(aws):
                    wv = []
                    if i == 0:
                        wv = [("hmm", lh)]
                    if nm == "woT":
                        wv = [("ymm", ly)]
                    I("p", lambda nm=nm, dstb=dstb, woff=woff: e.dma_start(
                        out=sl3(T[dstb][:, woff:woff + 8192], 8, 0, D),
                        in_=awT[nm].rearrange("(c p) d -> p c d", p=128)),
                      sem="wld", key=("aw", nm), n=16, waits=wv)
                I("s", lambda: e.activation(T["ld0sb"][:, 1:2], T["ld0sb"][:, 0:1],
                                            AF.Exp, scale=-1.0,
                                            bias=T["ld0sb"][:, 6:7]),
                  sem="gs", key=("mk1",),
                  waits=[("cld", M(("cld0",))), ("s0", ls0),
                         ("gp", M(("zb0",)))])
                I("v", lambda: e.tensor_scalar(T["ld0sb"][:, 2:3],
                                               T["ld0sb"][:, 1:2], 1.0, None,
                                               ALU.add),
                  sem="gv", key=("mk2",), waits=[("gs", M(("mk1",)))])
                I("v", lambda: e.reciprocal(T["ld0sb"][:, 3:4], T["ld0sb"][:, 2:3]),
                  sem="gv", key=("mk3",), waits=[("gv", M(("mk2",)))])
                I("s", lambda: e.activation(T["ld0sb"][:, 4:5], T["ld0sb"][:, 3:4],
                                            AF.Ln, bias=T["ld0sb"][:, 6:7]),
                  sem="gs", key=("mk4",), waits=[("gv", M(("mk3",)))])
                I("s", lambda: e.activation(T["masksb"][:, :], T["diffc"][:, :],
                                            AF.Exp, scale=T["ld0sb"][:, 4:5],
                                            bias=T["ld0sb"][:, 6:7]),
                  sem="gs", key=("mk5",), waits=[("gs", M(("mk4",)))])
                I("v", lambda: e.tensor_tensor(T["masksb"][:, :], T["masksb"][:, :],
                                               T["indc"][:, :], ALU.mult),
                  sem="gv", key=("mk6",), waits=[("gs", M(("mk5",)))])

                for gidx in range(16):
                    half = gidx // 8
                    key = (2, 0, gidx)
                    uslot = gidx % 2
                    xli = S8["xli"]; S8["xli"] += 1
                    rel = xrel.get(xli - 2)
                    xs = T["xsb"][:, (xli % 2) * 2048:(xli % 2) * 2048 + 2048]
                    I("q", lambda xs=xs, gidx=gidx:
                        e.dma_start(out=xs, in_=x1s[gidx, :, :]),
                      sem=XL(xli), key=("xl",) + key, n=16,
                      waits=([rel] if rel else []))
                    xb = (T["hbf"][:, 0:2048] if xli % 2 == 0 else T["xb16b"][:, :])
                    wx = [(XL(xli), M(("xl",) + key)),
                          ("gv", M(("b2e", gidx - 1))),
                          ("gt", M(("aomm", gidx - 1, 1)))]
                    if gidx < 2:
                        wx += [("ymm", ly), ("s0", ls0), ("pad", lpad),
                               ("stt", lstt)]
                    I("v", lambda xb=xb, xs=xs: e.tensor_copy(xb, xs),
                      sem="gv", key=("Axb",) + key, waits=wx)
                    eT = [("stt", lstt)] if gidx == 0 else []
                    eV = ([("pad", lpad)] if gidx == 0 else []) + \
                        [("gt", M(("aomm", gidx - 1, 1)))]
                    eP = ([("hmm", lh)] if gidx == 0 else []) + \
                        [("gt", M(("vmm", gidx - 2, 1)))]
                    if DBG2 and gidx < 2:
                        eP = eP + [("xst", M(("dbgu",)))]
                    ln_block(2, half, gidx, xs, xb, key, uslot,
                             xsem=XL(xli), extraT=eT, extraV=eV, extraP=eP)
                    if DBG2 and gidx == 0:
                        I("q", lambda: e.dma_start(out=dbgr[0, :, 0:512],
                                                   in_=T["bcsb"][:, :]),
                          sem="xst", key=("dbgr0",), n=16,
                          waits=[("gs", M(("Abce",) + key))])
                        I("q", lambda: e.dma_start(out=dbgr[1, :, :],
                                                   in_=T["rows"][:, :]),
                          sem="xst", key=("dbgr1",), n=16)
                    mark[("uxli2", gidx)] = xli
                    b2_grp(gidx, half, key, uslot, xli, xs, p2mb, ls0, lstt, lpad)
                mark[("attend",)] = cnt["gt"]
                mark[("attendv",)] = cnt["gv"]
                I("p", lambda: e.memset(T["rb"][:, :], 0.0),
                  sem="gp", key=("rb1",), waits=[("gv", cnt["gv"])])
                I("p", lambda: e.memset(T["mfm"][:, :], 0.0),
                  sem="gp", key=("mfmr", 3), waits=[("gt", cnt["gt"])])
                I("q", lambda: e.dma_start(out=T["sel"][:, :], in_=sel_in[:, :]),
                  sem="cld", key=("selr",), n=16, waits=[("gt", cnt["gt"])])

            def b2_grp(gidx, half, key, uslot, xli, xs, p2mb, ls0, lstt, lpad):
                def u_ap(k):
                    return T["usb"][:, k * 2048 + uslot * 256:
                                    k * 2048 + uslot * 256 + 256]
                def u_tile(k, t):
                    base = k * 2048 + uslot * 256 + t * 128
                    return T["usb"][:, base:base + 128]
                for nm, woff, dstb, ps in (("q", 0, "q_sb", hps),
                                           ("k", 8192, "k_sb", yps)):
                    for r in range(2):
                        def pmm(nm=nm, woff=woff, r=r, ps=ps):
                            ins = None
                            for dj in range(4):
                                dd = r * 4 + dj
                                for k in range(KC):
                                    ins = e.matmul(
                                        ps[:, dj * 256:(dj + 1) * 256],
                                        T["w1sb"][:, woff + k * 1024 + dd * 128:
                                                  woff + k * 1024 +
                                                  (dd + 1) * 128],
                                        u_ap(k), start=(k == 0),
                                        stop=(k == KC - 1))
                            return ins
                        ws = [("gp", M(("Au",) + key + (KC - 1,)))]
                        if r == 1:
                            ws.append(("gs", M((nm + "ev", gidx, 0))))
                        if nm == "q":
                            ws.append(("gv", M(("aoadd", gidx - 1, 0))))
                            if gidx == 0:
                                ws.append(("wld", M(("aw", "woT"))))
                                ws.append(("s0", ls0))
                        else:
                            ws.append(("gv", M(("zz", gidx - 1, 1))))
                            ws.append(("gv", M(("aoadd", gidx - 1, 1))))
                        I("t", pmm, sem="gt", key=(nm + "mm", gidx, r), waits=ws)
                        wse = [("gt", M((nm + "mm", gidx, r)))]
                        if gidx == 0 and r == 0:
                            wse += [("stt", lstt), ("mbcmm", p2mb),
                                    ("pad", lpad)]
                        I("s", lambda nm=nm, dstb=dstb, r=r, ps=ps: e.activation(
                            T[dstb][:, r * 1024:(r + 1) * 1024], ps[:, 0:1024],
                            AF.Identity, bias=T["ld0sb"][:, 6:7]),
                          sem="gs", key=(nm + "ev", gidx, r), waits=wse)
                for r in range(2):
                    def gmm(r=r):
                        ins = None
                        for dj in range(4):
                            dd = r * 4 + dj
                            for k in range(KC):
                                ins = e.matmul(
                                    yps[:, 1024 + dj * 256:1024 + (dj + 1) * 256],
                                    T["w1sb"][:, 24576 + k * 1024 + dd * 128:
                                              24576 + k * 1024 + (dd + 1) * 128],
                                    u_ap(k), start=(k == 0), stop=(k == KC - 1))
                        return ins
                    ws = [("gp", M(("Au",) + key + (KC - 1,)))]
                    if r == 0:
                        ws.append(("gs", M(("vev", gidx - 1, 1))))
                    else:
                        ws.append(("gs", M(("gex", gidx, 0))))
                    I("t", gmm, sem="gt", key=("gmm", gidx, r), waits=ws)
                    ws2 = [("gt", M(("gmm", gidx, r))),
                           ("gt", M(("Ast",) + key)),
                           ("gs", M(("mk5",))),
                           ("gv", M(("gah", gidx, r - 1)))]
                    I("s", lambda: e.activation(T["g1h"][:, :], yps[:, 1024:2048],
                                                AF.Exp, scale=-1.0,
                                                bias=T["ld0sb"][:, 6:7]),
                      sem="gs", key=("gex", gidx, r), waits=ws2)
                    I("v", lambda: e.tensor_scalar(T["gahb"][:, :], T["g1h"][:, :],
                                                   1.0, None, ALU.add),
                      sem="gv", key=("gah", gidx, r),
                      waits=[("gs", M(("gex", gidx, r))),
                             ("gv", M(("Axb",) + key))])
                    wgr = [("gv", M(("gah", gidx, r)))]
                    if gidx == 0 and r == 0:
                        wgr.append(("gt", M(("Alt", 1, 1, 7))))
                    def grcf(r=r):
                        with nc.allow_low_precision(reason="bf16 gate operand"):
                            return e.reciprocal(
                                T["g_sb"][:, r * 1024:(r + 1) * 1024],
                                T["gahb"][:, :])
                    I("v", grcf, sem="gv", key=("grc", gidx, r), waits=wgr)
                for t in range(2):
                    def vmm(t=t):
                        ins = None
                        for c in range(2):
                            for k in range(KC):
                                ins = e.matmul(
                                    yps[:, 1024 + c * 512:1024 + (c + 1) * 512],
                                    u_tile(k, t),
                                    T["w1sb"][:, 16384 + k * 1024 + c * 512:
                                              16384 + k * 1024 + (c + 1) * 512],
                                    start=(k == 0), stop=(k == KC - 1))
                        return ins
                    ws = [("gs", M(("gex", gidx, 1)) if t == 0 else
                           M(("vev", gidx, 0)))]
                    I("t", vmm, sem="gt", key=("vmm", gidx, t), waits=ws)
                    wse = [("gt", M(("vmm", gidx, t))),
                           ("gt", M(("omm", gidx, t - 1)) if t else
                            M(("omm", gidx - 1, 1)))]
                    if gidx == 0 and t == 0:
                        wse.append(("mbcmm", p2mb))
                    I("s", lambda: e.activation(T["v_sb"][:, :], yps[:, 1024:2048],
                                                AF.Identity,
                                                bias=T["ld0sb"][:, 6:7]),
                      sem="gs", key=("vev", gidx, t), waits=wse)

                    def stmm(t=t):
                        ins = None
                        for c in range(KC):
                            ins = e.matmul(
                                mbcps[:, t * 128:(t + 1) * 128],
                                T["k_sb"][:, c * 256 + t * 128:
                                          c * 256 + t * 128 + 128],
                                T["q_sb"][:, c * 256 + t * 128:
                                          c * 256 + t * 128 + 128],
                                start=(c == 0), stop=(c == KC - 1))
                        return ins
                    ws = [("gs", M(("qev", gidx, 1))), ("gs", M(("kev", gidx, 1))),
                          ("gv", M(("smul", gidx, 0)) if t else
                           M(("smul", gidx - 1, 1)))]
                    if gidx == 0 and t == 0:
                        ws.append(("mbce", cnt["mbce"]))
                    I("t", stmm, sem="gt", key=("stmm", gidx, t), waits=ws)
                    wsm = [("gt", M(("stmm", gidx, t))), ("gv", M(("mk6",)))]
                    if gidx == 0 and t == 0:
                        wsm.append(("gt", M(("Alt", 1, 1, 7))))
                    I("v", lambda t=t: e.tensor_tensor(
                        T["s_sb"][:, t * 128:(t + 1) * 128],
                        mbcps[:, t * 128:(t + 1) * 128], T["masksb"][:, :],
                        ALU.mult),
                      sem="gv", key=("smul", gidx, t), waits=wsm)

                    def omm(t=t):
                        ins = None
                        for dd in range(KC):
                            ins = e.matmul(
                                yps[:, dd * 128:(dd + 1) * 128],
                                T["v_sb"][:, dd * 128:(dd + 1) * 128],
                                T["s_sb"][:, t * 128:(t + 1) * 128],
                                start=True, stop=True)
                        return ins
                    ws = [("gs", M(("vev", gidx, t))), ("gv", M(("smul", gidx, t))),
                          ("gv", M(("zz", gidx, 0)) if t else
                           M(("zz", gidx - 1, 1)))]
                    I("t", omm, sem="gt", key=("omm", gidx, t), waits=ws)
                    wz = [("gt", M(("omm", gidx, t))), ("gv", M(("grc", gidx, 1))),
                          ("gv", M(("gah", gidx, 1)))]
                    if gidx == 0 and t == 0:
                        wz.append(("gt", M(("Ast",) + key)))
                    I("v", lambda t=t: e.tensor_tensor(
                        sl3(T["z_sb"][:, :], 8, t * 128, t * 128 + 128),
                        sl3(yps[:, 0:1024], 8, 0, 128),
                        sl3(T["g_sb"][:, :], 8, t * 128, t * 128 + 128),
                        ALU.mult),
                      sem="gv", key=("zz", gidx, t), waits=wz)
                u3 = sl3(T["usb"][:, :], 8, uslot * 256, uslot * 256 + 256)
                xs3 = sl3(xs, 8, 0, 256)
                I("v", lambda: e.tensor_tensor(xs3, xs3, u3, ALU.add),
                  sem="gv", key=("uadd", gidx),
                  waits=[(XL(xli), M(("xl",) + key)),
                         ("gp", M(("Au",) + key + (KC - 1,)))])
                for r in range(2):
                    def aomm(r=r):
                        ins = None
                        ps = hps if r == 0 else yps
                        for dj in range(4):
                            dd = r * 4 + dj
                            for k in range(KC):
                                ins = e.matmul(
                                    ps[:, dj * 256:(dj + 1) * 256],
                                    T["w2sb"][:, k * 1024 + dd * 128:
                                              k * 1024 + (dd + 1) * 128],
                                    T["z_sb"][:, k * 256:(k + 1) * 256],
                                    start=(k == 0), stop=(k == KC - 1))
                        return ins
                    ws = [("gv", M(("zz", gidx, 1)))]
                    if r == 0:
                        ws.append(("gs", M(("qev", gidx, 1))))
                        if gidx == 0:
                            ws.append(("wld", M(("aw", "woT"))))
                    else:
                        ws.append(("gv", M(("smul", gidx, 1))))
                    I("t", aomm, sem="gt", key=("aomm", gidx, r), waits=ws)
                    I("v", lambda r=r: e.tensor_tensor(
                        xs[:, r * 1024:(r + 1) * 1024],
                        xs[:, r * 1024:(r + 1) * 1024],
                        (hps if r == 0 else yps)[:, 0:1024], ALU.add),
                      sem="gv", key=("aoadd", gidx, r),
                      waits=[("gt", M(("aomm", gidx, r))),
                             ("gv", M(("uadd", gidx)))])
                mark[("b2e", gidx)] = M(("aoadd", gidx, 1))
                if DBG2 and gidx == 0:
                    for di, buf in enumerate(("q_sb", "k_sb", "v_sb", "s_sb",
                                              "g_sb", "z_sb")):
                        I("q", lambda di=di, buf=buf: e.dma_start(
                            out=dbg2[di, :, 0:T[buf].shape[1]], in_=T[buf][:, :]),
                          sem="xst", key=("dbg", di), n=16,
                          waits=[("gv", M(("b2e", gidx)))] if di == 0 else [])
                    I("q", lambda: e.dma_start(
                        out=dbg2[6, :, 0:2048],
                        in_=T["usb"][:, 0:2048]),
                      sem="xst", key=("dbg", 6), n=16)
                I("q", lambda xs=xs, gidx=gidx:
                    e.dma_start(out=x2s[gidx, :, :], in_=xs),
                  sem=XS(xli), key=("xs2", gidx), n=16,
                  waits=[("gv", M(("b2e", gidx)))])
                xrel[xli] = (XS(xli), M(("xs2", gidx)))

            const_loads()
            for pid in (1, 3):
                if pid == 3:
                    pass2()
                for half in (0, 1):
                    phaseA(pid, half)
                    phaseB(pid, half)
            if DBG2:
                dumps = [(dbg2[7, :, 0:16], T["onesc"][:, :]),
                         (dbg2[7, 0:1, 16:144], T["onesr"][0:1, 0:128]),
                         (dbg2[7, :, 256:768], T["rb"][:, :]),
                         (dbg2[7, :, 768:832], T["swlsb"][:, :])]
                for di, (dst, srcap) in enumerate(dumps):
                    I("q", lambda dst=dst, srcap=srcap:
                        e.dma_start(out=dst, in_=srcap),
                      sem="xst", key=("cdump", di), n=16,
                      waits=([("s0", cnt["s0"]), ("stt", cnt["stt"])]
                             if di == 0 else []))
            I("q", lambda: (e.wait_ge(SEM["xst"], cnt["xst"]),
                            e.wait_ge(SEM["xse"], cnt["xse"]))[0] or None)

        for EN, section in (("t", block.tensor), ("s", block.scalar),
                            ("v", block.vector), ("p", block.gpsimd),
                            ("q", block.sync)):
            section(lambda eng, EN=EN: walk(EN, eng))

    return nc


_CACHE = {}


def kernel(L0_attn_x, L0_attn_W_q, L0_attn_W_k, L0_attn_W_v, L0_attn_W_g,
           L0_attn_W_o, L0_attn_log_decay, L0_ffn_switch_w, L0_ffn_switch_b,
           L0_ffn_w1, L0_ffn_b1, L0_ffn_w2, L0_ffn_b2):
    x = np.asarray(L0_attn_x, dtype=np.float32)
    B, TT, Dm = x.shape
    tok = x.reshape(B * TT, Dm)
    per = tok.shape[0] // NCORES

    w1 = np.ascontiguousarray(np.asarray(L0_ffn_w1, dtype=np.float32))
    w2 = np.ascontiguousarray(np.asarray(L0_ffn_w2, dtype=np.float32))
    b1 = np.asarray(L0_ffn_b1, dtype=np.float32)
    b2 = np.asarray(L0_ffn_b2, dtype=np.float32)
    b1lh = np.ascontiguousarray(b1.reshape(NE, FT, 128).transpose(0, 2, 1))
    b2lh = np.ascontiguousarray(b2.reshape(NE, KC, 128).transpose(0, 2, 1))
    sw = np.asarray(L0_ffn_switch_w, dtype=np.float32)
    swl = np.ascontiguousarray(
        sw.reshape(KC, 128, 8).transpose(1, 0, 2)).reshape(128, 64)
    swb = np.ascontiguousarray(
        np.asarray(L0_ffn_switch_b, dtype=np.float32).reshape(8, 1))
    aw = {}
    for nm, W in (("wqT", L0_attn_W_q), ("wkT", L0_attn_W_k),
                  ("wvT", L0_attn_W_v), ("wgT", L0_attn_W_g),
                  ("woT", L0_attn_W_o)):
        aw[nm] = np.ascontiguousarray(np.asarray(W, dtype=np.float32).T)
    ld0 = np.full((128, 1), np.asarray(L0_attn_log_decay,
                                       dtype=np.float32)[0], dtype=np.float32)
    onesc = np.ones((128, 1), dtype=ml_dtypes.bfloat16)
    onesr = np.zeros((128, 128), dtype=ml_dtypes.bfloat16)
    onesr[0, :] = 1.0
    selc = np.zeros((128, 1024), dtype=ml_dtypes.bfloat16)
    for ex in range(8):
        selc[ex, ex * 128:(ex + 1) * 128] = 1.0
    identf = np.eye(128, dtype=np.float32)
    ii, jj = np.meshgrid(np.arange(128), np.arange(128), indexing="ij")
    same = (ii // 32) == (jj // 32)
    ge = jj >= ii
    diffc = np.where(same & ge, (jj - ii).astype(np.float32), 0.0
                     ).astype(np.float32)
    indc = (same & ge).astype(np.float32)

    if "nc" not in _CACHE:
        _CACHE["nc"] = build_program()
    nc = _CACHE["nc"]
    in_maps = []
    for c in range(NCORES):
        sl = tok[c * per:(c + 1) * per, :]
        fm = np.ascontiguousarray(sl.T)
        xt = np.ascontiguousarray(
            fm.reshape(KC, 128, 16, TG).transpose(2, 1, 0, 3)
        ).reshape(16, 128, KC * TG)
        in_maps.append({
            "xin": xt, "w1": w1, "w2": w2, "b1l": b1lh, "b2l": b2lh,
            "swli": swl, "swbi": swb,
            "wqT": aw["wqT"], "wkT": aw["wkT"], "wvT": aw["wvT"],
            "wgT": aw["wgT"], "woT": aw["woT"], "ld0i": ld0,
            "onesci": onesc, "onesri": onesr, "seli": selc,
            "identfi": identf, "diffi": diffc, "indi": indc,
        })
    res = run_bass_kernel_spmd(nc, in_maps, core_ids=list(range(NCORES)))
    outs = []
    for c in range(NCORES):
        o = np.asarray(res.results[c]["out"])
        fm = o.reshape(16, 128, KC, TG).transpose(2, 1, 0, 3).reshape(D, per)
        outs.append(np.ascontiguousarray(fm.T))
    full = np.concatenate(outs, axis=0).reshape(B, TT, Dm)
    return full.astype(np.float32)
